# version 10
# baseline (speedup 1.0000x reference)
import numpy as np
import concourse.bass as bass
import concourse.mybir as mybir
from concourse.bass_utils import run_bass_kernel_spmd
from concourse.bass_core_rust import IndirectOffsetOnAxis
from concourse.aot_env import allow_ici_all_to_all
from contextlib import ExitStack

F32 = mybir.dt.float32
BF16 = mybir.dt.bfloat16
I32 = mybir.dt.int32
AF = mybir.ActivationFunctionType
OP = mybir.AluOpType
AX = mybir.AxisListType

NC = 8
H = 4096
DIN = 8192
NST = 16
R = 256
T = 8192
TL = T // NC
DL = DIN // NC
NG = 8
TC = T // NG
HF = 512
EPS_LN = 1e-5
EPS_RMS = 1e-6
DEBUG = False
STAGE = 99


class KB:
    def __init__(self):
        self.nc = bass.Bass("TRN2", target_bir_lowering=False)
        self.ctx = ExitStack()
        self.sem = {}

    def new_sem(self, name):
        h = self.ctx.enter_context(self.nc.semaphore(name))
        self.sem[name] = [h, 0]
        return name

    def inc(self, inst, name, amt=1):
        e = self.sem[name]
        inst.then_inc(e[0], amt)
        e[1] += amt
        return e[1]

    def val(self, name):
        return self.sem[name][1]

    def w(self, eng, spec):
        if spec is None:
            return
        if isinstance(spec, tuple):
            spec = [spec]
        for nm, vv in spec:
            if vv > 0:
                eng.wait_ge(self.sem[nm][0], vv)

    def dma(self, eng, qsem, out, in_, waits=None):
        self.w(eng, waits)
        i = eng.dma_start(out=out, in_=in_)
        return self.inc(i, qsem, 16)


def build():
    kb = KB()
    nc = kb.nc
    sy, v, sc, t, g = nc.sync, nc.vector, nc.scalar, nc.tensor, nc.gpsimd
    W = kb.w

    def dri(name, shape, dt_=F32):
        return nc.dram_tensor(name, shape, dt_, kind="ExternalInput")

    hid_my = dri("hid_my", [TL, H])
    win_x = dri("win_x", [H, DL])
    win_z = dri("win_z", [H, DL])
    wxpT = dri("wxpT", [DL, 288])
    wdtT = dri("wdtT", [R, DL])
    woutT = dri("woutT", [DIN, H])
    conv_w = dri("conv_w", [128, 32])
    conv_b = dri("conv_b", [128, 8])
    dt_bias = dri("dt_bias", [128, 8])
    a_log = dri("a_log", [128, 8 * NST])
    d_prm = dri("d_prm", [128, 8])
    dt_ln = dri("dt_ln", [128, 2])
    b_ln = dri("b_ln", [NST, 1])
    c_ln = dri("c_ln", [NST, 1])
    up_w = dri("up_w", [NST, 256, 1024])
    dn_w = dri("dn_w", [NST, 1024, 256])
    perm_in = dri("perm_in", [H, 1], I32)

    out_my = nc.dram_tensor("out_my", [TL, H], F32, kind="ExternalOutput")

    ln1fm = nc.dram_tensor("ln1fm", [H, TL], BF16)
    ln1_all = nc.dram_tensor("ln1_all", [NC * H, TL], BF16)
    winx_bf = nc.dram_tensor("winx_bf", [8, 128, 4096], BF16)
    winz_bf = nc.dram_tensor("winz_bf", [8, 128, 4096], BF16)
    ssm_part = nc.dram_tensor("ssm_part", [NG, 288, TC], F32)
    ssm_all = nc.dram_tensor("ssm_all", [NG, 288, TC], F32)
    r_rows = nc.dram_tensor("r_rows", [NG, 3, TC], F32)
    brow = nc.dram_tensor("brow", [NG, NST, TC], BF16)
    crow = nc.dram_tensor("crow", [NG, NST, TC], BF16)
    xc_dr = nc.dram_tensor("xc_dr", [NG, DL, TC], BF16)
    zs_dr = nc.dram_tensor("zs_dr", [NG, DL, TC], F32)
    dt_dr = nc.dram_tensor("dt_dr", [NG, DL, TC], BF16)
    a2a_in = nc.dram_tensor("a2a_in", [NG, DIN, 128], BF16)
    a2a_out = nc.dram_tensor("a2a_out", [NG, DIN, 128], BF16)
    x2_dr = nc.dram_tensor("x2_dr", [TL, H], F32)
    ln2fm = nc.dram_tensor("ln2fm", [H, TL], BF16)
    ffn_dr = nc.dram_tensor("ffn_dr", [H, TL], F32)
    dbg = {}
    if DEBUG:
        for nm, shp, dt_ in [
            ("dbg_xc", [NG, DL, TC], BF16), ("dbg_dt", [NG, DL, TC], BF16),
            ("dbg_y", [NG, DIN, 128], BF16), ("dbg_ssm", [NG, 288, TC], F32),
            ("dbg_ln2", [H, TL], BF16), ("dbg_x2", [TL, H], F32),
            ("dbg_ffn", [H, TL], F32), ("dbg_ln1", [NC * H, TL], BF16),
            ("dbg_zs", [NG, DL, TC], F32), ("dbg_ln1my", [H, TL], BF16),
            ("dbg_brow", [NG, NST, TC], BF16), ("dbg_crow", [NG, NST, TC], BF16),
            ("dbg_rr", [NG, 3, TC], F32), ("dbg_dtrn", [NG, 128, 2 * TC], BF16),
        ]:
            dbg[nm] = nc.dram_tensor(nm, shp, dt_, kind="ExternalOutput")

    L = kb.new_sem("L")
    GQ = kb.new_sem("GQ")
    for _nm in ("GPAR", "GST0", "GST1", "LH", "LS1", "LRHS",
                "LW0", "LW1", "LXC0", "LXC1", "LZS0", "LZS1",
                "LSM0", "LSM1", "LRM", "LRMB", "LRR", "LRBD", "LRBC", "LBC",
                "LDT0", "LDT1", "LRP",
                "LSCD0", "LSCD1", "LSCX0", "LSCX1", "LSCZ0", "LSCZ1",
                "LYB0", "LYB1",
                "LYM", "GWO0", "GWO1", "LHT0", "LHT1", "LXW0", "LXW1",
                "LXR0", "LXR1", "GPM", "GFW",
                "GGA", "GSC", "LFT", "LXC20", "LXC21", "LOU0", "LOU1", "LT2"):
        kb.new_sem(_nm)
    PE = kb.new_sem("PE")
    V = kb.new_sem("V")
    AC = kb.new_sem("AC")
    PL = kb.new_sem("PL")
    CCAG = kb.new_sem("CCAG")
    CCAR = [kb.new_sem(f"CCAR{i}") for i in range(NG)]
    CCA2A = [kb.new_sem(f"CCA2A{i}") for i in range(NG)]

    with nc.Block() as block:
        pctx = ExitStack()
        consts = pctx.enter_context(nc.sbuf_tensor("consts", [128, 8], F32))
        mctx = ExitStack()
        sb = lambda n, s, d: mctx.enter_context(nc.sbuf_tensor(n, s, d))
        pst = lambda n, s, d: mctx.enter_context(nc.psum_tensor(n, s, d))

        iden = sb("iden", [128, 128], BF16)
        ones128 = sb("ones128", [128, 1], F32)
        ones16 = sb("ones16", [16, 1], F32)
        ones128b = sb("ones128b", [128, 1], BF16)
        ones16b = sb("ones16b", [16, 1], BF16)
        convd = sb("convd", [128, 32 * 128], BF16)
        dprd = sb("dprd", [128, 8 * 128], BF16)
        dtbd = sb("dtbd", [128, 8 * 128], BF16)
        ones512b = sb("ones512b", [128, 512], BF16)
        a_sb = sb("a_sb", [128, 8 * NST], F32)
        cw_sb = sb("cw_sb", [128, 32], F32)
        cb_sb = sb("cb_sb", [128, 8], F32)
        dtb_sb = sb("dtb_sb", [128, 8], F32)
        dpr_sb = sb("dpr_sb", [128, 8], F32)
        dtln_sb = sb("dtln_sb", [128, 2], F32)
        bln_sb = sb("bln_sb", [16, 1], F32)
        cln_sb = sb("cln_sb", [16, 1], F32)
        wxp_sb = sb("wxp_sb", [128, 8 * 288], BF16)
        wdt_sb = sb("wdt_sb", [128, 2 * DL], BF16)

        wstr = sb("wstr", [128, 2 * 4096], BF16)
        rhs = sb("rhs", [128, 32 * 512], BF16)
        xtr = sb("xtr", [128, 2 * 528], BF16)
        halo = sb("halo", [128, 8 * 4], BF16)
        xct = sb("xct", [128, 2 * 512], BF16)
        zct = sb("zct", [128, 2 * 512], F32)
        e_ev = sb("e_ev", [128, 2 * 512], F32)
        e01 = sb("e01", [128, 2 * TC], F32)
        ebc = sb("ebc", [16, TC], F32)
        sq01 = sb("sq01", [128, TC], BF16)
        sqbc = sb("sqbc", [16, TC], BF16)
        msr = sb("msr", [1, 512], F32)
        rro = sb("rro", [1, 512], F32)
        rstb = sb("rstb", [128, TC], F32)
        rstbc = sb("rstbc", [16, TC], F32)
        dtrn = sb("dtrn", [128, 2 * TC], BF16)
        bcn = sb("bcn", [16, TC], BF16)
        brep = sb("brep", [128, NST * HF], BF16)
        crep = sb("crep", [128, NST * HF], BF16)
        dtp_t = sb("dtp_t", [128, 2 * 512], BF16)
        dtt = sb("dtt", [128, 2 * HF], BF16)
        xcs = sb("xcs", [128, 2 * HF], BF16)
        zss = sb("zss", [128, 2 * HF], F32)
        dtxb = sb("dtxb", [128, 2 * HF], BF16)
        asc = sb("asc", [128, 4 * HF], F32)
        b0s = sb("b0s", [128, 4 * HF], F32)
        hsc = sb("hsc", [128, 2 * HF], BF16)
        carry = sb("carry", [128, 8 * NST], F32)
        ybf = sb("ybf", [128, 2 * HF], BF16)
        spt = sb("spt", [128, 2 * 512], F32)
        l1ctx = ExitStack()
        sbl1 = lambda n, s, d: l1ctx.enter_context(nc.sbuf_tensor(n, s, d))
        hidt = sbl1("hidt", [128, H], F32)
        stat = sbl1("stat", [128, 8], F32)
        trev = sbl1("trev", [128, 1024], BF16)

        pp = [pst("pp0", [128, 512], F32), pst("pp1", [128, 512], F32)]
        xpp = [pst("xp0", [128, 512], F32), pst("xp1", [128, 512], F32),
               pst("xp2", [128, 512], F32)]
        cvp = pst("cvp", [128, 512], F32)
        ptr_ctx = ExitStack()
        ptr = ptr_ctx.enter_context(nc.psum_tensor("ptr", [128, 1024], BF16))

        g.memset(iden[:, :], 0.0)
        g.affine_select(out=iden[:, :], in_=iden[:, :], compare_op=OP.not_equal,
                        fill=1.0, base=0, pattern=[[-1, 128]], channel_multiplier=1)
        g.memset(ones128[:, :], 1.0)
        g.memset(ones16[:, :], 1.0)
        g.memset(ones128b[:, :], 1.0)
        g.memset(ones16b[:, :], 1.0)
        g.memset(ones512b[:, :], 1.0)
        g.memset(halo[:, :], 0.0)
        cvals = [float(EPS_LN), float(EPS_RMS), 1.0 / R, 1.0 / NST, -1.0, 1.0]
        for ci, cv in enumerate(cvals):
            g.memset(consts[:, ci:ci + 1], cv)
            nc.const_aps.register(F32, cv, consts[:, ci:ci + 1])
        i = g.memset(carry[:, :], 0.0)
        gq_init = kb.inc(i, GQ, 16)

        kb.dma(g, "GPAR", cw_sb[:, :], conv_w.ap())
        kb.dma(g, "GPAR", cb_sb[:, :], conv_b.ap())
        kb.dma(g, "GPAR", dtb_sb[:, :], dt_bias.ap())
        kb.dma(g, "GPAR", dpr_sb[:, :], d_prm.ap())
        kb.dma(g, "GPAR", a_sb[:, :], a_log.ap())
        kb.dma(g, "GPAR", dtln_sb[:, :], dt_ln.ap())
        kb.dma(g, "GPAR", bln_sb[:, :], b_ln.ap())
        kb.dma(g, "GPAR", cln_sb[:, :], c_ln.ap())
        kb.dma(g, "GPAR", wxp_sb[:, :].rearrange("p (c e) -> p c e", e=288), wxpT.ap().rearrange("(c p) e -> p c e", p=128))
        gq_small = kb.dma(g, "GPAR", wdt_sb[:, :].rearrange("p (rc d) -> p rc d", d=DL), wdtT.ap().rearrange("(rc p) d -> p rc d", p=128))

        stg_st = {0: 0, 1: 0}
        for part, (srcw, dst) in enumerate([(win_x, winx_bf), (win_z, winz_bf)]):
            for dtile in range(8):
                half = dtile % 2
                ld = kb.dma(g, f"GST{half}",
                            wstr[:, half * 4096:(half + 1) * 4096]
                            .rearrange("p (k d) -> p k d", d=128),
                            srcw.ap()[:, dtile * 128:(dtile + 1) * 128]
                            .rearrange("(k p) d -> p k d", p=128),
                            waits=[(f"GST{half}", stg_st[half])])
                W(g, (f"GST{half}", ld))
                stg_st[half] = kb.dma(g, f"GST{half}", dst.ap()[dtile],
                                      wstr[:, half * 4096:(half + 1) * 4096])
        gq_stage = (kb.val("GST0"), kb.val("GST1"))

        W(sc, ("GPAR", gq_small))
        i = sc.activation(a_sb[:, :], a_sb[:, :], AF.Exp)
        ac1 = kb.inc(i, AC)
        W(sc, (AC, ac1))
        i = sc.mul(a_sb[:, :], a_sb[:, :], -1.0)
        ac_a = kb.inc(i, AC)

        W(v, ("GPAR", gq_small))
        for dc in range(8):
            for k in range(4):
                v.tensor_scalar_mul(convd[:, (dc * 4 + k) * 128:(dc * 4 + k + 1) * 128],
                                    iden[:, :], cw_sb[:, dc * 4 + k:dc * 4 + k + 1])
            v.tensor_scalar_mul(dprd[:, dc * 128:(dc + 1) * 128],
                                iden[:, :], dpr_sb[:, dc:dc + 1])
            i = v.tensor_scalar_mul(dtbd[:, dc * 128:(dc + 1) * 128],
                                    iden[:, :], dtb_sb[:, dc:dc + 1])
        v_diag = kb.inc(i, V)

        hid_free = 0
        ptr_free = 0
        trev_free = 0
        prev_tr = 0
        for tb in range(8):
            ld = kb.dma(sy, "LH", hidt[:, :], hid_my.ap()[tb * 128:(tb + 1) * 128, :],
                        waits=[("V", hid_free)])
            W(sc, [("LH", ld), ("V", hid_free), ("PE", prev_tr)])
            i = sc.activation(rhs[:, 0:H], hidt[:, :], AF.Square)
            ac_sq = kb.inc(i, AC)
            W(v, ("LH", ld))
            i = v.reduce_sum(stat[:, 0:1], hidt[:, :], AX.X)
            v_r0 = kb.inc(i, V)
            W(v, [("AC", ac_sq), ("V", v_r0)])
            i = v.reduce_sum(stat[:, 1:2], rhs[:, 0:H], AX.X)
            v_r1 = kb.inc(i, V)
            W(v, (V, v_r1))
            v.tensor_scalar_mul(stat[:, 2:3], stat[:, 0:1], 1.0 / H)
            i = v.tensor_scalar_mul(stat[:, 4:5], stat[:, 1:2], 1.0 / H)
            v_m = kb.inc(i, V)
            W(v, (V, v_m))
            i = v.tensor_mul(stat[:, 5:6], stat[:, 2:3], stat[:, 2:3])
            v_m2 = kb.inc(i, V)
            W(v, (V, v_m2))
            i = v.tensor_sub(stat[:, 6:7], stat[:, 4:5], stat[:, 5:6])
            v_var = kb.inc(i, V)
            W(sc, (V, v_var))
            i = sc.activation(stat[:, 7:8], stat[:, 6:7], AF.Sqrt, bias=float(EPS_LN))
            ac_sd = kb.inc(i, AC)
            W(v, (AC, ac_sd))
            i = v.reciprocal(stat[:, 3:4], stat[:, 7:8])
            v_rs = kb.inc(i, V)
            W(v, [("V", v_rs), ("PE", prev_tr)])
            i = v.tensor_scalar(rhs[:, 0:H], hidt[:, :], stat[:, 2:3], stat[:, 3:4],
                                OP.subtract, OP.mult)
            v_ln = kb.inc(i, V)
            hid_free = v_ln
            for grp in range(4):
                W(t, [("V", v_ln), ("AC", ptr_free)])
                for ii in range(8):
                    hc = grp * 8 + ii
                    i = t.transpose(ptr[:, ii * 128:(ii + 1) * 128],
                                    rhs[:, hc * 128:(hc + 1) * 128], iden[:, :])
                prev_tr = kb.inc(i, PE)
                W(sc, [("PE", prev_tr), ("LS1", trev_free)])
                i = sc.copy(trev[:, :], ptr[:, :])
                ptr_free = kb.inc(i, AC)
                trev_free = kb.dma(
                    sy, "LS1",
                    ln1fm.ap()[grp * 1024:(grp + 1) * 1024, tb * 128:(tb + 1) * 128]
                    .rearrange("(hc p) t -> p hc t", p=128),
                    trev[:, :].rearrange("p (hc t) -> p hc t", t=128),
                    waits=[("AC", ptr_free)])
        ln1_st = kb.val("LS1")
        ptr_ctx.close()
        yac = pst("yac", [128, 1024], F32)
        l1ctx.close()
        hcs = sb("hcs", [128, NST * HF], BF16)

        if STAGE >= 1:
            W(g, [("LS1", ln1_st), ("GST0", gq_stage[0]), ("GST1", gq_stage[1]), ("GPAR", gq_small)])
            i = g.collective_compute("AllGather", OP.bypass,
                                     replica_groups=[list(range(NC))],
                                     ins=[ln1fm.ap()], outs=[ln1_all.ap()])
            kb.inc(i, CCAG)
        else:
            W(g, [("LS1", ln1_st), ("GST0", gq_stage[0]), ("GST1", gq_stage[1])])

        rhs_free = 0
        w_free = {0: 0, 1: 0}
        pp_free = {0: 0, 1: 0}
        xpp_free = 0
        cvp_free = 0
        cvp_free_v = [0]
        yac_free = {0: 0, 1: 0}
        xtr_free = {0: 0, 1: 0}
        xct_free = {0: [], 1: []}
        zct_free = {0: 0, 1: 0}
        eev_free = 0
        eev1_free = [0]
        e01_free = 0
        ebc_free = 0
        sq01_free = 0
        sqbc_free = 0
        msr_free = 0
        rro_free = 0
        rstb_free = 0
        rstbc_free = 0
        bcn_free = 0
        dtrn_free = 0
        brep_free = 0
        crep_free = 0
        dtp_free = {0: [], 1: []}
        dtt_free = {0: [], 1: []}
        xcs_free = {0: [], 1: []}
        zss_free = {0: 0, 1: 0}
        ybf_free = {0: 0, 1: 0}
        asc_free = {n_: 0 for n_ in range(4)}
        b0_free = {n_: 0 for n_ in range(4)}
        hc_free = {n_: 0 for n_ in range(NST)}
        dtx_free = {0: [], 1: []}
        rms_marks = {}
        dt_ready = {}
        xc_done = {}
        zs_done = {}
        vdt_prev = 0

        def emit_inproj_half(gg, t2, pieces=None, sched=()):
            nonlocal rhs_free, xpp_free, cvp_free, eev_free
            if pieces is None:
                pieces = []
            if gg == 0 and t2 == 0:
                W(sy, (CCAG, 1))
            for k in range(32):
                kb.dma(sy, "LRHS", rhs[:, k * 512:(k + 1) * 512],
                       bass.AP(ln1_all, k * 128 * TL + t2 * 4 * H * TL + gg * 128,
                               [[TL, 128], [H * TL, 4], [1, 128]]),
                       waits=[("PE", rhs_free)] if k == 0 else None)
            rhs_ld = kb.val("LRHS")
            for dtile in range(16):
                part = 0 if dtile < 8 else 1
                dt8 = dtile % 8
                wbuf = dtile % 2
                src = winx_bf if part == 0 else winz_bf
                w_ld = kb.dma(sy, f"LW{wbuf}",
                              wstr[:, wbuf * 4096:(wbuf + 1) * 4096],
                              src.ap()[dt8],
                              waits=[("PE", w_free[wbuf])])
                ppb = dtile % 2
                W(t, [(f"LW{wbuf}", w_ld), ("LRHS", rhs_ld), ("AC", pp_free[ppb])])
                if gg == 0 and t2 == 0 and dtile == 0:
                    W(t, [(CCAG, 1), ("V", v_diag)])
                for k in range(32):
                    i = t.matmul(pp[ppb][:, :],
                                 wstr[:, wbuf * 4096 + k * 128:wbuf * 4096 + (k + 1) * 128],
                                 rhs[:, k * 512:(k + 1) * 512],
                                 start=(k == 0), stop=(k == 31))
                pe_mm = kb.inc(i, PE)
                w_free[wbuf] = pe_mm
                if dtile == 15:
                    rhs_free = pe_mm

                if part == 0:
                    xb = dt8 % 2
                    W(sc, [("PE", pe_mm), ("PE", xtr_free[xb])])
                    i = sc.copy(xtr[:, xb * 528 + 16:xb * 528 + 528], pp[ppb][:, :])
                    ac_x = kb.inc(i, AC)
                    pp_free[ppb] = ac_x
                    W(g, (AC, ac_x))
                    g.tensor_copy(xtr[:, xb * 528 + 13:xb * 528 + 16],
                                  halo[:, dt8 * 4:dt8 * 4 + 3])
                    i = g.tensor_copy(halo[:, dt8 * 4:dt8 * 4 + 3],
                                      xtr[:, xb * 528 + 525:xb * 528 + 528])
                    pl_h = kb.inc(i, PL)
                    W(t, [("PL", pl_h), ("AC", cvp_free), ("V", cvp_free_v[0])])
                    for k in range(4):
                        i = t.matmul(cvp[:, :],
                                     convd[:, (dt8 * 4 + k) * 128:(dt8 * 4 + k + 1) * 128],
                                     xtr[:, xb * 528 + 13 + k:xb * 528 + 525 + k],
                                     start=(k == 0), stop=(k == 3))
                    pe_cv = kb.inc(i, PE)
                    xtr_free[xb] = pe_cv
                    cb = dt8 % 2
                    W(sc, [("PE", pe_cv)] + xct_free[cb])
                    i = sc.activation(xct[:, cb * 512:(cb + 1) * 512], cvp[:, :],
                                      AF.Silu, bias=cb_sb[:, dt8:dt8 + 1])
                    ac_xc = kb.inc(i, AC)
                    cvp_free = ac_xc
                    st = kb.dma(sy, f"LXC{cb}",
                                xc_dr.ap()[gg, dt8 * 128:(dt8 + 1) * 128,
                                           t2 * 512:(t2 + 1) * 512],
                                xct[:, cb * 512:(cb + 1) * 512], waits=[("AC", ac_xc)])
                    W(t, (AC, ac_xc))
                    if dt8 == 0:
                        W(t, (AC, xpp_free))
                    for e in range(3):
                        ecols = 128 if e < 2 else 32
                        i = t.matmul(xpp[e][0:ecols, :],
                                     wxp_sb[:, dt8 * 288 + e * 128:dt8 * 288 + e * 128 + ecols],
                                     xct[:, cb * 512:(cb + 1) * 512],
                                     start=(dt8 == 0), stop=(dt8 == 7))
                    pe_xp = kb.inc(i, PE)
                    xct_free[cb] = [("PE", pe_xp), (f"LXC{cb}", st)]
                if (t2 * 16 + dtile) in sched and pieces:
                    pieces.pop(0)()
                if part == 1:
                    zb = dt8 % 2
                    W(sc, [("PE", pe_mm), (f"LZS{zb}", zct_free[zb])])
                    i = sc.activation(zct[:, zb * 512:(zb + 1) * 512], pp[ppb][:, :], AF.Silu)
                    ac_z = kb.inc(i, AC)
                    pp_free[ppb] = ac_z
                    zct_free[zb] = kb.dma(
                        sy, f"LZS{zb}", zs_dr.ap()[gg, dt8 * 128:(dt8 + 1) * 128,
                                          t2 * 512:(t2 + 1) * 512],
                        zct[:, zb * 512:(zb + 1) * 512], waits=[("AC", ac_z)])
            if t2 == 1:
                while pieces:
                    pieces.pop(0)()
            pe_now = kb.val(PE)
            W(sc, [("PE", pe_now), ("LSM0", eev_free), ("LSM1", eev1_free[0])])
            for e in range(2):
                i = sc.copy(e_ev[:, e * 512:(e + 1) * 512], xpp[e][:, :])
            ac_ssm = kb.inc(i, AC)
            st0 = kb.dma(sy, "LSM0",
                         ssm_part.ap()[gg, 0:128, t2 * 512:(t2 + 1) * 512],
                         e_ev[:, 0:512], waits=[("AC", ac_ssm)])
            eev1_free[0] = kb.dma(sy, "LSM1",
                   ssm_part.ap()[gg, 128:256, t2 * 512:(t2 + 1) * 512],
                   e_ev[:, 512:1024])
            W(sc, ("LSM0", st0))
            i = sc.copy(e_ev[0:32, 0:512], xpp[2][0:32, :])
            ac_ssm2 = kb.inc(i, AC)
            xpp_free = ac_ssm2
            eev_free = kb.dma(sy, "LSM0",
                              ssm_part.ap()[gg, 256:288, t2 * 512:(t2 + 1) * 512],
                              e_ev[0:32, 0:512], waits=[("AC", ac_ssm2)])

        def rms_half(gg, sec, n_feat, half, cvp_row):
            nonlocal msr_free, rro_free
            W(sc, [("AC", msr_free)])
            i = sc.copy(msr[0:1, :], cvp_row)
            ac_ms = kb.inc(i, AC)
            W(sc, [("LRR", rro_free), ("AC", ac_ms)])
            i = sc.activation(rro[0:1, :], msr[0:1, :], AF.Sqrt,
                              scale=1.0 / n_feat, bias=float(EPS_RMS))
            ac_rt = kb.inc(i, AC)
            msr_free = ac_rt
            W(v, (AC, ac_rt))
            i = v.reciprocal(rro[0:1, :], rro[0:1, :])
            v_rc = kb.inc(i, V)
            rro_free = kb.dma(sy, "LRR",
                              r_rows.ap()[gg, sec:sec + 1, half * 512:(half + 1) * 512],
                              rro[0:1, :], waits=[("V", v_rc)])
            return ac_ms

        def emit_rms(gg):
            nonlocal cvp_free, e01_free, ebc_free, sq01_free, sqbc_free
            nonlocal rstb_free, rstbc_free, bcn_free, msr_free
            W(sy, (CCAR[gg], 1))
            kb.dma(sy, "LRM", e01[:, 0:TC], ssm_all.ap()[gg, 0:128, :],
                   waits=[("V", e01_free)])
            l01 = kb.dma(sy, "LRM", e01[:, TC:2 * TC], ssm_all.ap()[gg, 128:256, :])
            ac_sq = {}
            pe_ms = 0
            for e in range(2):
                W(sc, [("LRM", l01), ("PE", sq01_free if e == 0 else pe_ms)])
                i = sc.square(sq01[:, :], e01[:, e * TC:(e + 1) * TC])
                ac_sq[e] = kb.inc(i, AC)
                W(t, [("AC", ac_sq[e]), ("AC", cvp_free if e == 0 else 0),
                      ("V", cvp_free_v[0] if e == 0 else 0), ("AC", msr_free)])
                t.matmul(cvp[0:1, 0:512], ones128b[:, :], sq01[:, 0:512],
                         start=(e == 0), stop=(e == 1))
                i = t.matmul(cvp[64:65, 0:512], ones128b[:, :], sq01[:, 512:1024],
                             start=(e == 0), stop=(e == 1))
                pe_ms = kb.inc(i, PE)
                if e == 1:
                    sq01_free = pe_ms
            W(sc, (PE, pe_ms))
            rms_half(gg, 0, R, 0, cvp[0:1, 0:512])
            ac_ms2 = rms_half(gg, 0, R, 1, cvp[64:65, 0:512])
            cvp_free = ac_ms2
            st_dt = rro_free
            bl1 = kb.dma(sy, "LRBD", rstb[:, :],
                         r_rows.ap()[gg, 0:1, :].partition_broadcast(128),
                         waits=[("LRR", st_dt), ("V", rstb_free)])
            W(v, ("LRBD", bl1))
            for e in range(2):
                W(v, (PE, dtrn_free) if e == 0 else None)
                i = v.scalar_tensor_tensor(dtrn[:, e * TC:(e + 1) * TC],
                                           e01[:, e * TC:(e + 1) * TC],
                                           dtln_sb[:, e:e + 1], rstb[:, :],
                                           OP.mult, OP.mult)
            v_dtn = kb.inc(i, V)
            e01_free = v_dtn
            rstb_free = v_dtn
            for sec, (lnw, dst) in enumerate([(bln_sb, brow), (cln_sb, crow)]):
                lbc = kb.dma(sy, "LRMB", ebc[:, :],
                             ssm_all.ap()[gg, 256 + sec * NST:256 + (sec + 1) * NST, :],
                             waits=[("V", ebc_free)])
                W(sc, [("LRMB", lbc), ("PE", sqbc_free)])
                i = sc.square(sqbc[:, :], ebc[:, :])
                ac_sqb = kb.inc(i, AC)
                W(t, [("AC", ac_sqb), ("AC", cvp_free), ("V", cvp_free_v[0]), ("AC", msr_free)])
                t.matmul(cvp[0:1, 0:512], ones16b[:, :], sqbc[:, 0:512],
                         start=True, stop=True)
                i = t.matmul(cvp[64:65, 0:512], ones16b[:, :], sqbc[:, 512:1024],
                             start=True, stop=True)
                pe_msb = kb.inc(i, PE)
                sqbc_free = pe_msb
                W(sc, (PE, pe_msb))
                rms_half(gg, 1 + sec, NST, 0, cvp[0:1, 0:512])
                ac_ms2 = rms_half(gg, 1 + sec, NST, 1, cvp[64:65, 0:512])
                cvp_free = ac_ms2
                st_bc = rro_free
                bl = kb.dma(sy, "LRBC", rstbc[:, :],
                            r_rows.ap()[gg, 1 + sec:2 + sec, :].partition_broadcast(16),
                            waits=[("LRR", st_bc), ("V", rstbc_free)])
                W(v, [("LRBC", bl), ("LBC", bcn_free)])
                i = v.scalar_tensor_tensor(bcn[:, :], ebc[:, :],
                                           lnw[:, :], rstbc[:, :], OP.mult, OP.mult)
                v_bcn = kb.inc(i, V)
                rstbc_free = v_bcn
                ebc_free = v_bcn
                bcn_free = kb.dma(sy, "LBC", dst.ap()[gg], bcn[:, :], waits=[("V", v_bcn)])
            rms_marks[gg] = dict(v_dtn=v_dtn, brow_st=bcn_free)

        def emit_dtproj_part(gg, dcs):
            nonlocal cvp_free, dtrn_free, vdt_prev
            m = rms_marks[gg]
            for dc in dcs:
                for t2 in range(2):
                    dbuf = t2
                    W(t, [("V", m["v_dtn"]), ("AC", cvp_free), ("V", cvp_free_v[0])])
                    for rc in range(2):
                        t.matmul(cvp[:, :],
                                 wdt_sb[:, rc * DL + dc * 128:rc * DL + (dc + 1) * 128],
                                 dtrn[:, rc * TC + t2 * 512:rc * TC + (t2 + 1) * 512],
                                 start=(rc == 0), stop=False)
                    i = t.matmul(cvp[:, :], dtbd[:, dc * 128:(dc + 1) * 128],
                                 ones512b[:, :], start=False, stop=True)
                    pe_dt = kb.inc(i, PE)
                    dtrn_free = pe_dt
                    W(sc, [("PE", pe_dt), ("V", vdt_prev)])
                    i = sc.activation(spt[:, 512:1024], cvp[:, :], AF.Abs)
                    a1 = kb.inc(i, AC)
                    cvp_free = a1
                    W(sc, (AC, a1))
                    i = sc.activation(spt[:, 512:1024], spt[:, 512:1024], AF.Exp, scale=-1.0)
                    a2 = kb.inc(i, AC)
                    W(sc, (AC, a2))
                    i = sc.activation(spt[:, 512:1024], spt[:, 512:1024], AF.Ln, bias=1.0)
                    ac_l = kb.inc(i, AC)
                    W(v, [("AC", ac_l), ("PE", pe_dt)] + dtp_free[dbuf])
                    i = v.tensor_scalar_max(spt[:, 0:512], cvp[:, :], 0.0)
                    v_mx = kb.inc(i, V)
                    cvp_free_v[0] = v_mx
                    W(v, (V, v_mx))
                    i = v.tensor_add(dtp_t[:, dbuf * 512:(dbuf + 1) * 512],
                                     spt[:, 0:512], spt[:, 512:1024])
                    vdt_prev = kb.inc(i, V)
                    st = kb.dma(sy, f"LDT{dbuf}",
                                dt_dr.ap()[gg, dc * 128:(dc + 1) * 128,
                                           t2 * 512:(t2 + 1) * 512],
                                dtp_t[:, dbuf * 512:(dbuf + 1) * 512],
                                waits=[("V", vdt_prev)])
                    dtp_free[dbuf] = [(f"LDT{dbuf}", st)]
            if 7 in dcs:
                dt_ready[gg] = (kb.val("LDT0"), kb.val("LDT1"))

        def emit_dtproj(gg):
            emit_dtproj_part(gg, range(0, 8))

        def make_scan_pieces(gg):
            st_ = {"rep_ld": 0}
            pieces = []

            def piece(hf, dc):
                nonlocal brep_free, crep_free
                db = dc % 2
                if dc == 0:
                    m = rms_marks[gg]
                    W(sy, [("LBC", m["brow_st"]), ("PL", brep_free), ("V", crep_free)])
                    for n in range(NST):
                        kb.dma(sy, "LRP", brep[:, n * HF:(n + 1) * HF],
                               brow.ap()[gg, n:n + 1, hf * HF:(hf + 1) * HF]
                               .partition_broadcast(128))
                    for n in range(NST):
                        kb.dma(sy, "LRP", crep[:, n * HF:(n + 1) * HF],
                               crow.ap()[gg, n:n + 1, hf * HF:(hf + 1) * HF]
                               .partition_broadcast(128))
                    st_["rep_ld"] = kb.val("LRP")
                rep_ld = st_["rep_ld"]
                if True:
                    dl_ = kb.dma(sy, f"LSCD{db}", dtt[:, db * HF:(db + 1) * HF],
                                 dt_dr.ap()[gg, dc * 128:(dc + 1) * 128,
                                            hf * HF:(hf + 1) * HF],
                                 waits=[("LDT0", dt_ready[gg][0]),
                                        ("LDT1", dt_ready[gg][1]),
                                        ("LXC0", xc_done[gg][0]),
                                        ("LXC1", xc_done[gg][1]),
                                        ("LZS0", zs_done[gg][0]),
                                        ("LZS1", zs_done[gg][1])] + dtt_free[db])
                    xl_ = kb.dma(sy, f"LSCX{db}", xcs[:, db * HF:(db + 1) * HF],
                                 xc_dr.ap()[gg, dc * 128:(dc + 1) * 128,
                                            hf * HF:(hf + 1) * HF],
                                 waits=xcs_free[db])
                    zl_ = kb.dma(sy, f"LSCZ{db}", zss[:, db * HF:(db + 1) * HF],
                                 zs_dr.ap()[gg, dc * 128:(dc + 1) * 128,
                                            hf * HF:(hf + 1) * HF],
                                 waits=[("V", zss_free[db])])
                    W(v, [(f"LSCD{db}", dl_), (f"LSCX{db}", xl_)] + dtx_free[db])
                    i = v.tensor_mul(dtxb[:, db * HF:(db + 1) * HF],
                                     dtt[:, db * HF:(db + 1) * HF],
                                     xcs[:, db * HF:(db + 1) * HF])
                    v_dtx = kb.inc(i, V)
                    ac_agen_last = 0
                    pl_b0_last = 0
                    v_ym_last = 0
                    for n in range(NST):
                        ab = n % 2
                        as_ = n % 4
                        W(sc, [(f"LSCD{db}", dl_), ("V", asc_free[as_])])
                        i = sc.activation(asc[:, as_ * HF:(as_ + 1) * HF],
                                          dtt[:, db * HF:(db + 1) * HF], AF.Exp,
                                          scale=a_sb[:, dc * NST + n:dc * NST + n + 1])
                        ac_agen_last = kb.inc(i, AC)
                        W(g, [("V", v_dtx), ("LRP", rep_ld), ("V", b0_free[as_])])
                        i = g.tensor_mul(b0s[:, as_ * HF:(as_ + 1) * HF],
                                         dtxb[:, db * HF:(db + 1) * HF],
                                         brep[:, n * HF:(n + 1) * HF])
                        pl_b0_last = kb.inc(i, PL)
                        W(v, [("AC", ac_agen_last), ("PL", pl_b0_last)])
                        i = v.tensor_tensor_scan(hsc[:, ab * HF:(ab + 1) * HF],
                                                 asc[:, as_ * HF:(as_ + 1) * HF],
                                                 b0s[:, as_ * HF:(as_ + 1) * HF],
                                                 carry[:, dc * NST + n:dc * NST + n + 1],
                                                 OP.mult, OP.add)
                        v_h = kb.inc(i, V)
                        asc_free[as_] = v_h
                        b0_free[as_] = v_h
                        W(v, (V, v_h))
                        v.tensor_copy(carry[:, dc * NST + n:dc * NST + n + 1],
                                      hsc[:, (ab + 1) * HF - 1:(ab + 1) * HF])
                        W(v, [("PE", hc_free[n]), ("LRP", rep_ld)])
                        i = v.tensor_mul(hcs[:, n * HF:(n + 1) * HF],
                                         hsc[:, ab * HF:(ab + 1) * HF],
                                         crep[:, n * HF:(n + 1) * HF])
                        v_ym_last = kb.inc(i, V)
                        W(t, (V, v_ym_last))
                        if n == 0:
                            W(t, [("V", yac_free[db]), (f"LSCX{db}", xl_)])
                        i = t.matmul(yac[:, db * 512:(db + 1) * 512], iden[:, :],
                                     hcs[:, n * HF:(n + 1) * HF],
                                     start=(n == 0), stop=False)
                        pe_y = kb.inc(i, PE)
                        hc_free[n] = pe_y
                    i = t.matmul(yac[:, db * 512:(db + 1) * 512],
                                 dprd[:, dc * 128:(dc + 1) * 128],
                                 xcs[:, db * HF:(db + 1) * HF],
                                 start=False, stop=True)
                    pe_yd = kb.inc(i, PE)
                    xcs_free[db] = [("PE", pe_yd), ("V", v_dtx)]
                    dtt_free[db] = [("AC", ac_agen_last), ("V", v_dtx)]
                    dtx_free[db] = [("PL", pl_b0_last)]
                    W(v, [("PE", pe_yd), (f"LSCZ{db}", zl_), (f"LYB{db}", ybf_free[db])])
                    i = v.tensor_mul(ybf[:, db * HF:(db + 1) * HF],
                                     yac[:, db * 512:(db + 1) * 512],
                                     zss[:, db * HF:(db + 1) * HF])
                    v_y = kb.inc(i, V)
                    yac_free[db] = v_y
                    zss_free[db] = v_y
                    W(v, (V, v_y))
                    ybf_free[db] = kb.dma(
                        sc, f"LYB{db}",
                        bass.AP(a2a_in, gg * DIN * 128 + dc * 128 * 128 + hf * 4 * DL * 128,
                                [[128, 128], [DL * 128, 4], [1, 128]]),
                        ybf[:, db * HF:(db + 1) * HF], waits=[("V", v_y)])
                if dc == 7:
                    brep_free = pl_b0_last
                    crep_free = v_ym_last
                if hf == 1 and dc == 7:
                    W(g, [("LYB0", kb.val("LYB0")), ("LYB1", kb.val("LYB1"))])
                    with allow_ici_all_to_all():
                        i2 = g.collective_compute(
                            "AllToAll", OP.bypass,
                            replica_groups=[list(range(NC))],
                            ins=[a2a_in.ap()[gg]], outs=[a2a_out.ap()[gg]])
                    kb.inc(i2, CCA2A[gg])

            for hf in range(2):
                for dc in range(8):
                    pieces.append(lambda hf=hf, dc=dc: piece(hf, dc))
            return pieces

        for gg in range(NG if STAGE >= 2 else 0):
            pieces = []
            if gg >= 1 and STAGE >= 3:
                pieces.append(lambda gp=gg - 1: emit_rms(gp))
                pieces.append(lambda gp=gg - 1: emit_dtproj_part(gp, range(0, 4)))
                pieces.append(lambda gp=gg - 1: emit_dtproj_part(gp, range(4, 8)))
                if STAGE >= 4:
                    pieces += make_scan_pieces(gg - 1)
            np_ = len(pieces)
            sched = {d for d in range(32)
                     if (d + 1) * np_ // 32 > d * np_ // 32}
            emit_inproj_half(gg, 0, pieces, sched)
            emit_inproj_half(gg, 1, pieces, sched)
            xc_done[gg] = (kb.val("LXC0"), kb.val("LXC1"))
            zs_done[gg] = (kb.val("LZS0"), kb.val("LZS1"))
            W(g, [("LSM0", kb.val("LSM0")), ("LSM1", kb.val("LSM1"))])
            i = g.collective_compute("AllReduce", OP.add, replica_groups=[list(range(NC))],
                                     ins=[ssm_part.ap()[gg]], outs=[ssm_all.ap()[gg]])
            kb.inc(i, CCAR[gg])
        if STAGE >= 3:
            emit_rms(NG - 1)
            emit_dtproj(NG - 1)
        if STAGE >= 4:
            for p in make_scan_pieces(NG - 1):
                p()

        if DEBUG:
            for eng_nm in ("LXC0", "LXC1", "LZS0", "LZS1", "LDT0", "LDT1",
                           "LYB0", "LYB1", "LS1", "LBC", "LRR"):
                W(sy, (eng_nm, kb.val(eng_nm)))
            if STAGE >= 1:
                W(sy, (CCAG, 1))
            for gg2 in range(NG if STAGE >= 2 else 0):
                W(sy, (CCAR[gg2], 1))
            for nm, srct in [("dbg_xc", xc_dr), ("dbg_dt", dt_dr), ("dbg_y", a2a_in),
                             ("dbg_ssm", ssm_all), ("dbg_ln1", ln1_all),
                             ("dbg_zs", zs_dr), ("dbg_ln1my", ln1fm),
                             ("dbg_brow", brow), ("dbg_crow", crow),
                             ("dbg_rr", r_rows)]:
                kb.dma(sy, L, dbg[nm].ap(), srct.ap())

        mix_sems = ["PE", "V", "AC", "PL", "GQ", "L", "GPAR", "GST0", "GST1", "LH",
                    "LS1", "LRHS", "LW0", "LW1", "LXC0", "LXC1", "LZS0",
                    "LZS1", "LSM0", "LSM1", "LRM", "LRMB", "LRR", "LRBD",
                    "LRBC", "LBC", "LDT0", "LDT1", "LRP", "LSCD0", "LSCD1",
                    "LSCX0", "LSCX1", "LSCZ0", "LSCZ1", "LYB0", "LYB1"]
        mix = {nm: kb.val(nm) for nm in mix_sems}
        mctx.close()

        tctx = ExitStack()
        sb2 = lambda n, s, d: tctx.enter_context(nc.sbuf_tensor(n, s, d))

        iden2 = sb2("iden2", [128, 128], BF16)
        idenf2 = sb2("idenf2", [128, 128], F32)
        ymine = sb2("ymine", [128, 64 * 1024], BF16)
        wo_t = sb2("wo_t", [128, 2 * 512], BF16)
        hid_t = sb2("hid_t", [128, 2 * 512], F32)
        x2t = sb2("x2t", [128, 2 * 512], F32)
        x2sq = sb2("x2sq", [128, 512], BF16)
        ln2st = sb2("ln2st", [128, 8 * 8], F32)
        ln2t = sb2("ln2t", [128, 512], BF16)
        trev2 = sb2("trev2", [128, 512], BF16)
        perm_sb = sb2("perm_sb", [128, 32], I32)
        ln2p = sb2("ln2p", [128, 2 * 1024], BF16)
        upw = sb2("upw", [128, 2048], BF16)
        dnw = sb2("dnw", [128, 2048], BF16)
        t2sb = sb2("t2sb", [128, 8 * 1024], BF16)
        ffnev = sb2("ffnev", [128, 2 * 1024], F32)
        ffnt = sb2("ffnt", [128, 1024], F32)
        x2c = sb2("x2c", [128, 2 * 1024], F32)
        outt = sb2("outt", [128, 2 * 1024], F32)

        for eng in (sy, v, sc, t, g):
            W(eng, [(nm, mix[nm]) for nm in mix_sems])
            if STAGE >= 4:
                for gg in range(NG):
                    W(eng, (CCA2A[gg], 1))

        g.memset(iden2[:, :], 0.0)
        g.affine_select(out=iden2[:, :], in_=iden2[:, :], compare_op=OP.not_equal,
                        fill=1.0, base=0, pattern=[[-1, 128]], channel_multiplier=1)
        g.memset(idenf2[:, :], 0.0)
        i = g.affine_select(out=idenf2[:, :], in_=idenf2[:, :], compare_op=OP.not_equal,
                            fill=1.0, base=0, pattern=[[-1, 128]], channel_multiplier=1)
        gq_id2 = kb.inc(i, GQ, 16)
        for gi in range(32):
            kb.dma(g, "GPM", perm_sb[:, gi:gi + 1], perm_in.ap()[gi * 128:(gi + 1) * 128, :])
        gq_perm = kb.val("GPM")

        for gg in range(NG if STAGE >= 5 else 0):
            kb.dma(sy, "LYM",
                   ymine[:, :].rearrange("p (dcc tt) -> p dcc tt", tt=1024)
                   [:, :, gg * 128:(gg + 1) * 128],
                   bass.AP(a2a_out, gg * DIN * 128, [[128, 128], [16384, 64], [1, 128]]))
        ym_ld = kb.val("LYM")

        opctx = ExitStack()
        op = [opctx.enter_context(nc.psum_tensor(f"op{i_}", [128, 512], F32))
              for i_ in range(8)]
        op_free = {i_: 0 for i_ in range(8)}
        x2sq_free = [0]
        wo_free = {0: 0, 1: 0}
        hid_free2 = {0: 0, 1: 0}
        x2t_free = {0: 0, 1: 0}
        pe_op = 0
        for hb in range(8 if STAGE >= 5 else 0):
            for dc in range(64):
                wb = dc % 2
                wl = kb.dma(g, f"GWO{wb}", wo_t[:, wb * 512:(wb + 1) * 512],
                            woutT.ap()[dc * 128:(dc + 1) * 128, hb * 512:(hb + 1) * 512],
                            waits=[("PE", wo_free[wb])])
                W(t, [(f"GWO{wb}", wl), ("LYM", ym_ld)])
                for tt in range(8):
                    if dc == 0:
                        W(t, (V, op_free[tt]))
                    i = t.matmul(op[tt][:, :],
                                 ymine[:, dc * 1024 + tt * 128:dc * 1024 + (tt + 1) * 128],
                                 wo_t[:, wb * 512:(wb + 1) * 512],
                                 start=(dc == 0), stop=(dc == 63))
                pe_op = kb.inc(i, PE)
                wo_free[wb] = pe_op
            for tt in range(8):
                hbuf = tt % 2
                hl = kb.dma(sy, f"LHT{hbuf}", hid_t[:, hbuf * 512:(hbuf + 1) * 512],
                            hid_my.ap()[tt * 128:(tt + 1) * 128, hb * 512:(hb + 1) * 512],
                            waits=[("V", hid_free2[hbuf])])
                W(v, [("PE", pe_op), (f"LHT{hbuf}", hl), (f"LXW{hbuf}", x2t_free[hbuf])])
                i = v.tensor_add(x2t[:, hbuf * 512:(hbuf + 1) * 512], op[tt][:, :],
                                 hid_t[:, hbuf * 512:(hbuf + 1) * 512])
                v_x2 = kb.inc(i, V)
                hid_free2[hbuf] = v_x2
                op_free[tt] = v_x2
                W(sc, [("V", v_x2), ("V", x2sq_free[0])])
                i = sc.activation(x2sq[:, :], x2t[:, hbuf * 512:(hbuf + 1) * 512],
                                  AF.Square)
                ac_q = kb.inc(i, AC)
                i = v.reduce_sum(ln2st[:, tt * 8 + 5:tt * 8 + 6],
                                 x2t[:, hbuf * 512:(hbuf + 1) * 512], AX.X)
                v_s1 = kb.inc(i, V)
                W(v, [("AC", ac_q), ("V", v_s1)])
                i = v.reduce_sum(ln2st[:, tt * 8 + 4:tt * 8 + 5], x2sq[:, :], AX.X)
                v_s2 = kb.inc(i, V)
                x2sq_free[0] = v_s2
                W(v, (V, v_s2))
                if hb == 0:
                    v.tensor_copy(ln2st[:, tt * 8 + 0:tt * 8 + 1],
                                  ln2st[:, tt * 8 + 5:tt * 8 + 6])
                    i = v.tensor_copy(ln2st[:, tt * 8 + 1:tt * 8 + 2],
                                      ln2st[:, tt * 8 + 4:tt * 8 + 5])
                else:
                    v.tensor_add(ln2st[:, tt * 8 + 0:tt * 8 + 1],
                                 ln2st[:, tt * 8 + 0:tt * 8 + 1],
                                 ln2st[:, tt * 8 + 5:tt * 8 + 6])
                    i = v.tensor_add(ln2st[:, tt * 8 + 1:tt * 8 + 2],
                                     ln2st[:, tt * 8 + 1:tt * 8 + 2],
                                     ln2st[:, tt * 8 + 4:tt * 8 + 5])
                v_st = kb.inc(i, V)
                x2t_free[hbuf] = kb.dma(
                    sy, f"LXW{hbuf}", x2_dr.ap()[tt * 128:(tt + 1) * 128, hb * 512:(hb + 1) * 512],
                    x2t[:, hbuf * 512:(hbuf + 1) * 512], waits=[("V", v_st)])
        x2_all_st = (kb.val("LXW0"), kb.val("LXW1"))

        W(v, (V, kb.val("V")))
        for tt in range(8 if STAGE >= 5 else 0):
            c0 = tt * 8
            v.tensor_scalar_mul(ln2st[:, c0 + 2:c0 + 3], ln2st[:, c0 + 0:c0 + 1], 1.0 / H)
            i = v.tensor_scalar_mul(ln2st[:, c0 + 3:c0 + 4], ln2st[:, c0 + 1:c0 + 2], 1.0 / H)
            v_f = kb.inc(i, V)
            W(v, (V, v_f))
            i = v.tensor_mul(ln2st[:, c0 + 6:c0 + 7], ln2st[:, c0 + 2:c0 + 3],
                             ln2st[:, c0 + 2:c0 + 3])
            v_f2 = kb.inc(i, V)
            W(v, (V, v_f2))
            i = v.tensor_sub(ln2st[:, c0 + 7:c0 + 8], ln2st[:, c0 + 3:c0 + 4],
                             ln2st[:, c0 + 6:c0 + 7])
            v_v = kb.inc(i, V)
            W(sc, (V, v_v))
            i = sc.activation(ln2st[:, c0 + 6:c0 + 7], ln2st[:, c0 + 7:c0 + 8],
                              AF.Sqrt, bias=float(EPS_LN))
            ac_s = kb.inc(i, AC)
            W(v, (AC, ac_s))
            i = v.reciprocal(ln2st[:, c0 + 7:c0 + 8], ln2st[:, c0 + 6:c0 + 7])
            kb.inc(i, V)
        v_ln2fin = kb.val(V)

        opctx.close()
        p2ctx = ExitStack()
        ptr2 = p2ctx.enter_context(nc.psum_tensor("ptr2", [128, 512], BF16))
        pfm = [p2ctx.enter_context(nc.psum_tensor("pfm0", [128, 512], F32)),
               p2ctx.enter_context(nc.psum_tensor("pfm1", [128, 512], F32))]
        pt2 = p2ctx.enter_context(nc.psum_tensor("pt2", [128, 1024], F32))

        ptr2_free = 0
        trev2_free = 0
        x2f_free = {0: 0, 1: 0}
        ln2t_pe = 0
        for tt in range(8 if STAGE >= 5 else 0):
            for hb in range(8):
                fb = hb % 2
                xl = kb.dma(sy, f"LXR{fb}", hid_t[:, fb * 512:(fb + 1) * 512],
                            x2_dr.ap()[tt * 128:(tt + 1) * 128, hb * 512:(hb + 1) * 512],
                            waits=[("V", x2f_free[fb]), ("LXW0", x2_all_st[0]),
                                   ("LXW1", x2_all_st[1]), ("V", hid_free2[fb])])
                W(v, [(f"LXR{fb}", xl), ("V", v_ln2fin), ("PE", ln2t_pe)])
                i = v.tensor_scalar(ln2t[:, :], hid_t[:, fb * 512:(fb + 1) * 512],
                                    ln2st[:, tt * 8 + 2:tt * 8 + 3],
                                    ln2st[:, tt * 8 + 7:tt * 8 + 8],
                                    OP.subtract, OP.mult)
                v_n = kb.inc(i, V)
                x2f_free[fb] = v_n
                W(t, [("V", v_n), ("AC", ptr2_free)])
                for s in range(4):
                    i = t.transpose(ptr2[:, s * 128:(s + 1) * 128],
                                    ln2t[:, s * 128:(s + 1) * 128], iden2[:, :])
                ln2t_pe = kb.inc(i, PE)
                W(sc, [("PE", ln2t_pe), ("LT2", trev2_free)])
                i = sc.copy(trev2[:, :], ptr2[:, :])
                ptr2_free = kb.inc(i, AC)
                trev2_free = kb.dma(
                    sy, "LT2",
                    ln2fm.ap()[hb * 512:(hb + 1) * 512, tt * 128:(tt + 1) * 128]
                    .rearrange("(s p) t -> p s t", p=128),
                    trev2[:, :].rearrange("p (s t) -> p s t", t=128),
                    waits=[("AC", ptr2_free)])
        ln2_st = kb.val("LT2")
        if DEBUG:
            kb.dma(sy, L, dbg["dbg_ln2"].ap(), ln2fm.ap(), waits=[("LT2", ln2_st)])
            kb.dma(sy, L, dbg["dbg_x2"].ap(), x2_dr.ap(), waits=[("LXW0", x2_all_st[0]), ("LXW1", x2_all_st[1])])

        pfm_free = [[], []]
        ffn_sc_done = [0]
        gath_free = {0: 0, 1: 0}
        upw_free = {0: 0, 1: 0}
        dnw_free = {0: 0, 1: 0}
        t2_free = 0
        ffn_ev_free = {0: 0, 1: 0}
        for h in range(NST if STAGE >= 6 else 0):
            ul = kb.dma(g, "GFW", upw[:, :].rearrange("p (kc m) -> p kc m", m=1024),
                        up_w.ap()[h].rearrange("(kc p) m -> p kc m", p=128),
                        waits=[("PE", upw_free[0]), ("GPM", gq_perm)])
            dl_ = kb.dma(g, "GFW", dnw[:, :].rearrange("p (kc d) -> p kc d", d=256),
                         dn_w.ap()[h].rearrange("(kc p) d -> p kc d", p=128),
                         waits=[("PE", dnw_free[0])])
            for kc in range(2):
                W(g, [("LT2", ln2_st), ("PE", gath_free[0])])
                i = g.indirect_dma_start(
                    out=ln2p[:, kc * 1024:(kc + 1) * 1024],
                    out_offset=None,
                    in_=ln2fm.ap()[0:128, :],
                    in_offset=IndirectOffsetOnAxis(perm_sb[:, 2 * h + kc:2 * h + kc + 1], 0))
                kb.inc(i, "GGA", 16)
            gl = kb.val("GGA")
            for mc in range(8):
                for t2 in range(2):
                    pb = (mc * 2 + t2) % 2
                    W(t, [("GGA", gl), ("GFW", dl_)] + pfm_free[pb])
                    for kc in range(2):
                        i = t.matmul(pfm[pb][:, :],
                                     upw[:, kc * 1024 + mc * 128:kc * 1024 + (mc + 1) * 128],
                                     ln2p[:, kc * 1024 + t2 * 512:kc * 1024 + (t2 + 1) * 512],
                                     start=(kc == 0), stop=(kc == 1))
                    pe_u = kb.inc(i, PE)
                    if mc == 7 and t2 == 1:
                        upw_free[0] = pe_u
                        gath_free[0] = pe_u
                    W(sc, [("PE", pe_u), ("PE", t2_free)])
                    i = sc.activation(t2sb[:, mc * 1024 + t2 * 512:mc * 1024 + (t2 + 1) * 512],
                                      pfm[pb][:, :], AF.Gelu)
                    ac_g2 = kb.inc(i, AC)
                    pfm_free[pb] = [("AC", ac_g2)]
            ac_t2 = kb.val(AC)
            for dsub in range(2):
                for t2 in range(2):
                    pb = (dsub * 2 + t2) % 2
                    W(t, [("AC", ac_t2), ("GFW", dl_), ("GSC", ffn_sc_done[0])]
                      + pfm_free[pb])
                    for kc in range(8):
                        i = t.matmul(pfm[pb][:, :],
                                     dnw[:, kc * 256 + dsub * 128:kc * 256 + (dsub + 1) * 128],
                                     t2sb[:, kc * 1024 + t2 * 512:kc * 1024 + (t2 + 1) * 512],
                                     start=(kc == 0), stop=(kc == 7))
                    pe_d = kb.inc(i, PE)
                    if dsub == 1 and t2 == 1:
                        t2_free = pe_d
                        dnw_free[0] = pe_d
                    W(v, (PE, pe_d))
                    i = v.tensor_copy(ffnev[:, dsub * 1024 + t2 * 512:
                                            dsub * 1024 + (t2 + 1) * 512],
                                      pfm[pb][:, :])
                    v_e = kb.inc(i, V)
                    pfm_free[pb] = [("V", v_e)]
            v_last = kb.val(V)
            for dsub in range(2):
                W(g, (V, v_last))
                i = g.indirect_dma_start(
                    out=ffn_dr.ap()[0:128, :],
                    out_offset=IndirectOffsetOnAxis(perm_sb[:, 2 * h + dsub:2 * h + dsub + 1], 0),
                    in_=ffnev[:, dsub * 1024:(dsub + 1) * 1024],
                    in_offset=None)
                kb.inc(i, "GSC", 16)
            ffn_sc_done[0] = kb.val("GSC")
        ffn_done = kb.val("GSC")
        if DEBUG:
            kb.dma(sy, L, dbg["dbg_ffn"].ap(), ffn_dr.ap(), waits=[("GSC", ffn_done)])

        pt2_free = 0
        ffnt_free = {0: 0, 1: 0}
        x2c_free = {0: 0, 1: 0}
        outt_free = {0: 0, 1: 0}
        for cc2 in range(32 if STAGE >= 6 else 0):
            fb = cc2 % 2
            fl = kb.dma(sy, "LFT", ffnt[:, :],
                        ffn_dr.ap()[cc2 * 128:(cc2 + 1) * 128, :],
                        waits=[("GSC", ffn_done), ("PE", ffnt_free[0])])
            W(t, [("LFT", fl), ("V", pt2_free)])
            for tt in range(8):
                i = t.transpose(pt2[:, tt * 128:(tt + 1) * 128],
                                ffnt[:, tt * 128:(tt + 1) * 128],
                                idenf2[:, :])
            pe_ft = kb.inc(i, PE)
            ffnt_free[0] = pe_ft
            xl = kb.dma(sy, f"LXC2{fb}",
                        x2c[:, fb * 1024:(fb + 1) * 1024]
                        .rearrange("p (tt c) -> p tt c", c=128),
                        x2_dr.ap()[:, cc2 * 128:(cc2 + 1) * 128]
                        .rearrange("(tt p) c -> p tt c", p=128),
                        waits=[("V", x2c_free[fb])])
            W(v, [("PE", pe_ft), (f"LXC2{fb}", xl), (f"LOU{fb}", outt_free[fb])])
            i = v.tensor_add(outt[:, fb * 1024:(fb + 1) * 1024], pt2[:, :],
                             x2c[:, fb * 1024:(fb + 1) * 1024])
            v_o = kb.inc(i, V)
            pt2_free = v_o
            x2c_free[fb] = v_o
            W(v, (V, v_o))
            outt_free[fb] = kb.dma(
                sy, f"LOU{fb}",
                out_my.ap()[:, cc2 * 128:(cc2 + 1) * 128]
                .rearrange("(tt p) c -> p tt c", p=128),
                outt[:, fb * 1024:(fb + 1) * 1024]
                .rearrange("p (tt c) -> p tt c", c=128),
                waits=[("V", v_o)])
        for _nm in ("LOU0", "LOU1", "LXW0", "LXW1", "LT2", "LS1"):
            W(sy, (_nm, kb.val(_nm)))
        for _nm in ("GSC", "GGA", "GFW", "GWO0", "GWO1", "GPM"):
            W(g, (_nm, kb.val(_nm)))
        p2ctx.close()
        tctx.close()
        pctx.close()
    kb.ctx.close()
    return kb


def kernel(L0_attn_hidden_states, L0_attn_in_proj_w, L0_attn_conv_w, L0_attn_conv_b,
           L0_attn_x_proj_w, L0_attn_dt_proj_w, L0_attn_dt_bias, L0_attn_A_log,
           L0_attn_D_param, L0_attn_out_proj_w, L0_attn_dt_ln_w, L0_attn_B_ln_w,
           L0_attn_C_ln_w, L0_ffn_up, L0_ffn_down, L0_ffn_perm):
    hid = np.ascontiguousarray(np.asarray(L0_attn_hidden_states, dtype=np.float32)
                               .reshape(T, H))
    win = np.asarray(L0_attn_in_proj_w, dtype=np.float32)
    wxpT_h = np.ascontiguousarray(np.asarray(L0_attn_x_proj_w, dtype=np.float32).T)
    wdtT_h = np.ascontiguousarray(np.asarray(L0_attn_dt_proj_w, dtype=np.float32).T)
    woutT_h = np.ascontiguousarray(np.asarray(L0_attn_out_proj_w, dtype=np.float32).T)
    convw = np.ascontiguousarray(np.asarray(L0_attn_conv_w, dtype=np.float32))
    convb = np.asarray(L0_attn_conv_b, dtype=np.float32).reshape(DIN, 1)
    dtb = np.asarray(L0_attn_dt_bias, dtype=np.float32).reshape(DIN, 1)
    alog = np.ascontiguousarray(np.asarray(L0_attn_A_log, dtype=np.float32))
    dprm = np.asarray(L0_attn_D_param, dtype=np.float32).reshape(DIN, 1)
    dtln = np.asarray(L0_attn_dt_ln_w, dtype=np.float32).reshape(R, 1)
    bln = np.asarray(L0_attn_B_ln_w, dtype=np.float32).reshape(NST, 1)
    cln = np.asarray(L0_attn_C_ln_w, dtype=np.float32).reshape(NST, 1)
    upw_h = np.ascontiguousarray(np.asarray(L0_ffn_up, dtype=np.float32))
    dnw_h = np.ascontiguousarray(np.asarray(L0_ffn_down, dtype=np.float32))
    perm = np.ascontiguousarray(np.asarray(L0_ffn_perm, dtype=np.int32).reshape(H, 1))

    hid_blocks = hid.reshape(64, 128, H)
    kb = build()
    in_maps = []
    for k in range(NC):
        in_maps.append({
            "hid_my": np.ascontiguousarray(hid_blocks[k::8].reshape(TL, H)),
            "win_x": np.ascontiguousarray(win[:, k * DL:(k + 1) * DL]),
            "win_z": np.ascontiguousarray(win[:, DIN + k * DL:DIN + (k + 1) * DL]),
            "wxpT": np.ascontiguousarray(wxpT_h[k * DL:(k + 1) * DL, :]),
            "wdtT": np.ascontiguousarray(wdtT_h[:, k * DL:(k + 1) * DL]),
            "woutT": woutT_h,
            "conv_w": np.ascontiguousarray(
                convw[k * DL:(k + 1) * DL, :].reshape(8, 128, 4)
                .transpose(1, 0, 2).reshape(128, 32)),
            "conv_b": np.ascontiguousarray(
                convb[k * DL:(k + 1) * DL, 0].reshape(8, 128).T),
            "dt_bias": np.ascontiguousarray(
                dtb[k * DL:(k + 1) * DL, 0].reshape(8, 128).T),
            "a_log": np.ascontiguousarray(
                alog[k * DL:(k + 1) * DL, :].reshape(8, 128, NST)
                .transpose(1, 0, 2).reshape(128, 8 * NST)),
            "d_prm": np.ascontiguousarray(
                dprm[k * DL:(k + 1) * DL, 0].reshape(8, 128).T),
            "dt_ln": np.ascontiguousarray(dtln[:, 0].reshape(2, 128).T),
            "b_ln": bln, "c_ln": cln,
            "up_w": upw_h, "dn_w": dnw_h, "perm_in": perm,
        })
    res = run_bass_kernel_spmd(kb.nc, in_maps, core_ids=list(range(NC)))
    out = np.zeros((64, 128, H), dtype=np.float32)
    for k in range(NC):
        out[k::8] = np.asarray(res.results[k]["out_my"]).reshape(8, 128, H)
    return out.reshape(1, T, H)


# version 11 (final)
# speedup vs baseline: 1.1780x; 1.1780x over previous
import numpy as np
import concourse.bass as bass
import concourse.mybir as mybir
from concourse.bass_utils import run_bass_kernel_spmd
from concourse.bass_core_rust import IndirectOffsetOnAxis
from concourse.aot_env import allow_ici_all_to_all
from contextlib import ExitStack

F32 = mybir.dt.float32
BF16 = mybir.dt.bfloat16
I32 = mybir.dt.int32
AF = mybir.ActivationFunctionType
OP = mybir.AluOpType
AX = mybir.AxisListType

NC = 8
H = 4096
DIN = 8192
NST = 16
R = 256
T = 8192
TL = T // NC
DL = DIN // NC
NG = 8
TC = T // NG
HF = 512
EPS_LN = 1e-5
EPS_RMS = 1e-6
DEBUG = False
STAGE = 99


class KB:
    def __init__(self):
        self.nc = bass.Bass("TRN2", target_bir_lowering=False)
        self.ctx = ExitStack()
        self.sem = {}

    def new_sem(self, name):
        h = self.ctx.enter_context(self.nc.semaphore(name))
        self.sem[name] = [h, 0]
        return name

    def inc(self, inst, name, amt=1):
        e = self.sem[name]
        inst.then_inc(e[0], amt)
        e[1] += amt
        return e[1]

    def val(self, name):
        return self.sem[name][1]

    def w(self, eng, spec):
        if spec is None:
            return
        if isinstance(spec, tuple):
            spec = [spec]
        for nm, vv in spec:
            if vv > 0:
                eng.wait_ge(self.sem[nm][0], vv)

    def dma(self, eng, qsem, out, in_, waits=None):
        self.w(eng, waits)
        i = eng.dma_start(out=out, in_=in_)
        return self.inc(i, qsem, 16)


def build():
    kb = KB()
    nc = kb.nc
    sy, v, sc, t, g = nc.sync, nc.vector, nc.scalar, nc.tensor, nc.gpsimd
    W = kb.w

    def dri(name, shape, dt_=F32):
        return nc.dram_tensor(name, shape, dt_, kind="ExternalInput")

    hid_my = dri("hid_my", [TL, H])
    win_x = dri("win_x", [H, DL])
    win_z = dri("win_z", [H, DL])
    wxpT = dri("wxpT", [DL, 288])
    wdtT = dri("wdtT", [R, DL])
    woutT = dri("woutT", [DIN, H])
    conv_w = dri("conv_w", [128, 32])
    conv_b = dri("conv_b", [128, 8])
    dt_bias = dri("dt_bias", [128, 8])
    a_log = dri("a_log", [128, 8 * NST])
    d_prm = dri("d_prm", [128, 8])
    dt_ln = dri("dt_ln", [128, 2])
    b_ln = dri("b_ln", [NST, 1])
    c_ln = dri("c_ln", [NST, 1])
    up_w = dri("up_w", [NST, 256, 1024])
    dn_w = dri("dn_w", [NST, 1024, 256])
    perm_in = dri("perm_in", [H, 1], I32)

    out_my = nc.dram_tensor("out_my", [TL, H], F32, kind="ExternalOutput")

    ln1fm = nc.dram_tensor("ln1fm", [H, TL], BF16)
    ln1_all = nc.dram_tensor("ln1_all", [NC * H, TL], BF16)
    winx_bf = nc.dram_tensor("winx_bf", [8, 128, 4096], BF16)
    winz_bf = nc.dram_tensor("winz_bf", [8, 128, 4096], BF16)
    ssm_part = nc.dram_tensor("ssm_part", [NG, 288, TC], F32)
    ssm_all = nc.dram_tensor("ssm_all", [NG, 288, TC], F32)
    r_rows = nc.dram_tensor("r_rows", [NG, 3, TC], F32)
    brow = nc.dram_tensor("brow", [NG, NST, TC], BF16)
    crow = nc.dram_tensor("crow", [NG, NST, TC], BF16)
    xc_dr = nc.dram_tensor("xc_dr", [NG, DL, TC], BF16)
    zs_dr = nc.dram_tensor("zs_dr", [NG, DL, TC], F32)
    dt_dr = nc.dram_tensor("dt_dr", [NG, DL, TC], BF16)
    a2a_in = nc.dram_tensor("a2a_in", [NG, DIN, 128], BF16)
    a2a_out = nc.dram_tensor("a2a_out", [NG, DIN, 128], BF16)
    x2_dr = nc.dram_tensor("x2_dr", [TL, H], F32)
    ln2fm = nc.dram_tensor("ln2fm", [H, TL], BF16)
    ffn_dr = nc.dram_tensor("ffn_dr", [H, TL], F32)
    dbg = {}
    if DEBUG:
        for nm, shp, dt_ in [
            ("dbg_xc", [NG, DL, TC], BF16), ("dbg_dt", [NG, DL, TC], BF16),
            ("dbg_y", [NG, DIN, 128], BF16), ("dbg_ssm", [NG, 288, TC], F32),
            ("dbg_ln2", [H, TL], BF16), ("dbg_x2", [TL, H], F32),
            ("dbg_ffn", [H, TL], F32), ("dbg_ln1", [NC * H, TL], BF16),
            ("dbg_zs", [NG, DL, TC], F32), ("dbg_ln1my", [H, TL], BF16),
            ("dbg_brow", [NG, NST, TC], BF16), ("dbg_crow", [NG, NST, TC], BF16),
            ("dbg_rr", [NG, 3, TC], F32), ("dbg_dtrn", [NG, 128, 2 * TC], BF16),
        ]:
            dbg[nm] = nc.dram_tensor(nm, shp, dt_, kind="ExternalOutput")

    L = kb.new_sem("L")
    GQ = kb.new_sem("GQ")
    for _nm in ("GPAR", "GST0", "GST1", "LH", "LS1", "LRHS",
                "LW0", "LW1", "LXC0", "LXC1", "LZS0", "LZS1",
                "LSM0", "LSM1", "LRM", "LRMB", "LRR", "LRBD", "LRBC", "LBC",
                "LDT0", "LDT1", "LRP",
                "LSCD0", "LSCD1", "LSCX0", "LSCX1", "LSCZ0", "LSCZ1",
                "LYB0", "LYB1",
                "LYM", "GWO0", "GWO1", "LHT0", "LHT1", "LXW0", "LXW1",
                "LXR0", "LXR1", "GPM", "GFW",
                "GGA", "GSC", "LFT", "LXC20", "LXC21", "LOU0", "LOU1", "LT2"):
        kb.new_sem(_nm)
    PE = kb.new_sem("PE")
    V = kb.new_sem("V")
    AC = kb.new_sem("AC")
    PL = kb.new_sem("PL")
    CCAG = kb.new_sem("CCAG")
    CCAR = [kb.new_sem(f"CCAR{i}") for i in range(NG)]
    CCA2A = [kb.new_sem(f"CCA2A{i}") for i in range(NG)]

    with nc.Block() as block:
        pctx = ExitStack()
        consts = pctx.enter_context(nc.sbuf_tensor("consts", [128, 8], F32))
        mctx = ExitStack()
        sb = lambda n, s, d: mctx.enter_context(nc.sbuf_tensor(n, s, d))
        pst = lambda n, s, d: mctx.enter_context(nc.psum_tensor(n, s, d))

        iden = sb("iden", [128, 128], BF16)
        ones128 = sb("ones128", [128, 1], F32)
        ones16 = sb("ones16", [16, 1], F32)
        ones128b = sb("ones128b", [128, 1], BF16)
        ones16b = sb("ones16b", [16, 1], BF16)
        convd = sb("convd", [128, 32 * 128], BF16)
        dprd = sb("dprd", [128, 8 * 128], BF16)
        dtbd = sb("dtbd", [128, 8 * 128], BF16)
        ones512b = sb("ones512b", [128, 512], BF16)
        a_sb = sb("a_sb", [128, 8 * NST], F32)
        cw_sb = sb("cw_sb", [128, 32], F32)
        cb_sb = sb("cb_sb", [128, 8], F32)
        dtb_sb = sb("dtb_sb", [128, 8], F32)
        dpr_sb = sb("dpr_sb", [128, 8], F32)
        dtln_sb = sb("dtln_sb", [128, 2], F32)
        bln_sb = sb("bln_sb", [16, 1], F32)
        cln_sb = sb("cln_sb", [16, 1], F32)
        wxp_sb = sb("wxp_sb", [128, 8 * 288], BF16)
        wdt_sb = sb("wdt_sb", [128, 2 * DL], BF16)

        wstr = sb("wstr", [128, 2 * 4096], BF16)
        rhs = sb("rhs", [128, 32 * 512], BF16)
        xtr = sb("xtr", [128, 2 * 528], BF16)
        halo = sb("halo", [128, 8 * 4], BF16)
        xct = sb("xct", [128, 2 * 512], BF16)
        zct = sb("zct", [128, 2 * 512], F32)
        e_ev = sb("e_ev", [128, 2 * 512], F32)
        e01 = sb("e01", [128, 2 * TC], F32)
        ebc = sb("ebc", [16, TC], F32)
        sq01 = sb("sq01", [128, TC], BF16)
        sqbc = sb("sqbc", [16, TC], BF16)
        msr = sb("msr", [1, 512], F32)
        rro = sb("rro", [1, 512], F32)
        rstb = sb("rstb", [128, TC], F32)
        rstbc = sb("rstbc", [16, TC], F32)
        dtrn = sb("dtrn", [128, 2 * TC], BF16)
        bcn = sb("bcn", [16, TC], BF16)
        brep = sb("brep", [128, NST * HF], BF16)
        crep = sb("crep", [128, NST * HF], BF16)
        dtp_t = sb("dtp_t", [128, 2 * 512], BF16)
        dtt = sb("dtt", [128, 2 * HF], BF16)
        xcs = sb("xcs", [128, 2 * HF], BF16)
        zss = sb("zss", [128, 2 * HF], F32)
        dtxb = sb("dtxb", [128, 2 * HF], BF16)
        asc = sb("asc", [128, 4 * HF], F32)
        b0s = sb("b0s", [128, 4 * HF], F32)
        hsc = sb("hsc", [128, 2 * HF], BF16)
        carry = sb("carry", [128, 8 * NST], F32)
        ybf = sb("ybf", [128, 2 * HF], BF16)
        spt = sb("spt", [128, 2 * 512], F32)
        l1ctx = ExitStack()
        sbl1 = lambda n, s, d: l1ctx.enter_context(nc.sbuf_tensor(n, s, d))
        hidt = sbl1("hidt", [128, H], F32)
        stat = sbl1("stat", [128, 8], F32)
        trev = sbl1("trev", [128, 1024], BF16)

        pp = [pst("pp0", [128, 512], F32), pst("pp1", [128, 512], F32)]
        xpp = [pst("xp0", [128, 512], F32), pst("xp1", [128, 512], F32),
               pst("xp2", [128, 512], F32)]
        cvp = pst("cvp", [128, 512], F32)
        ptr_ctx = ExitStack()
        ptr = ptr_ctx.enter_context(nc.psum_tensor("ptr", [128, 1024], BF16))

        g.memset(iden[:, :], 0.0)
        g.affine_select(out=iden[:, :], in_=iden[:, :], compare_op=OP.not_equal,
                        fill=1.0, base=0, pattern=[[-1, 128]], channel_multiplier=1)
        g.memset(ones128[:, :], 1.0)
        g.memset(ones16[:, :], 1.0)
        g.memset(ones128b[:, :], 1.0)
        g.memset(ones16b[:, :], 1.0)
        g.memset(ones512b[:, :], 1.0)
        g.memset(halo[:, :], 0.0)
        cvals = [float(EPS_LN), float(EPS_RMS), 1.0 / R, 1.0 / NST, -1.0, 1.0]
        for ci, cv in enumerate(cvals):
            g.memset(consts[:, ci:ci + 1], cv)
            nc.const_aps.register(F32, cv, consts[:, ci:ci + 1])
        i = g.memset(carry[:, :], 0.0)
        gq_init = kb.inc(i, GQ, 16)

        kb.dma(g, "GPAR", cw_sb[:, :], conv_w.ap())
        kb.dma(g, "GPAR", cb_sb[:, :], conv_b.ap())
        kb.dma(g, "GPAR", dtb_sb[:, :], dt_bias.ap())
        kb.dma(g, "GPAR", dpr_sb[:, :], d_prm.ap())
        kb.dma(g, "GPAR", a_sb[:, :], a_log.ap())
        kb.dma(g, "GPAR", dtln_sb[:, :], dt_ln.ap())
        kb.dma(g, "GPAR", bln_sb[:, :], b_ln.ap())
        kb.dma(g, "GPAR", cln_sb[:, :], c_ln.ap())
        kb.dma(g, "GPAR", wxp_sb[:, :].rearrange("p (c e) -> p c e", e=288), wxpT.ap().rearrange("(c p) e -> p c e", p=128))
        gq_small = kb.dma(g, "GPAR", wdt_sb[:, :].rearrange("p (rc d) -> p rc d", d=DL), wdtT.ap().rearrange("(rc p) d -> p rc d", p=128))

        stg_st = {0: 0, 1: 0}
        for part, (srcw, dst) in enumerate([(win_x, winx_bf), (win_z, winz_bf)]):
            for dtile in range(8):
                half = dtile % 2
                ld = kb.dma(g, f"GST{half}",
                            wstr[:, half * 4096:(half + 1) * 4096]
                            .rearrange("p (k d) -> p k d", d=128),
                            srcw.ap()[:, dtile * 128:(dtile + 1) * 128]
                            .rearrange("(k p) d -> p k d", p=128),
                            waits=[(f"GST{half}", stg_st[half])])
                W(g, (f"GST{half}", ld))
                stg_st[half] = kb.dma(g, f"GST{half}", dst.ap()[dtile],
                                      wstr[:, half * 4096:(half + 1) * 4096])
        gq_stage = (kb.val("GST0"), kb.val("GST1"))

        W(sc, ("GPAR", gq_small))
        i = sc.activation(a_sb[:, :], a_sb[:, :], AF.Exp)
        ac1 = kb.inc(i, AC)
        W(sc, (AC, ac1))
        i = sc.mul(a_sb[:, :], a_sb[:, :], -1.0)
        ac_a = kb.inc(i, AC)

        W(v, ("GPAR", gq_small))
        for dc in range(8):
            for k in range(4):
                v.tensor_scalar_mul(convd[:, (dc * 4 + k) * 128:(dc * 4 + k + 1) * 128],
                                    iden[:, :], cw_sb[:, dc * 4 + k:dc * 4 + k + 1])
            v.tensor_scalar_mul(dprd[:, dc * 128:(dc + 1) * 128],
                                iden[:, :], dpr_sb[:, dc:dc + 1])
            i = v.tensor_scalar_mul(dtbd[:, dc * 128:(dc + 1) * 128],
                                    iden[:, :], dtb_sb[:, dc:dc + 1])
        v_diag = kb.inc(i, V)

        hid_free = 0
        ptr_free = 0
        trev_free = 0
        prev_tr = 0
        for tb in range(8):
            ld = kb.dma(sy, "LH", hidt[:, :], hid_my.ap()[tb * 128:(tb + 1) * 128, :],
                        waits=[("V", hid_free)])
            W(sc, [("LH", ld), ("V", hid_free), ("PE", prev_tr)])
            i = sc.activation(rhs[:, 0:H], hidt[:, :], AF.Square)
            ac_sq = kb.inc(i, AC)
            W(v, ("LH", ld))
            i = v.reduce_sum(stat[:, 0:1], hidt[:, :], AX.X)
            v_r0 = kb.inc(i, V)
            W(v, [("AC", ac_sq), ("V", v_r0)])
            i = v.reduce_sum(stat[:, 1:2], rhs[:, 0:H], AX.X)
            v_r1 = kb.inc(i, V)
            W(v, (V, v_r1))
            v.tensor_scalar_mul(stat[:, 2:3], stat[:, 0:1], 1.0 / H)
            i = v.tensor_scalar_mul(stat[:, 4:5], stat[:, 1:2], 1.0 / H)
            v_m = kb.inc(i, V)
            W(v, (V, v_m))
            i = v.tensor_mul(stat[:, 5:6], stat[:, 2:3], stat[:, 2:3])
            v_m2 = kb.inc(i, V)
            W(v, (V, v_m2))
            i = v.tensor_sub(stat[:, 6:7], stat[:, 4:5], stat[:, 5:6])
            v_var = kb.inc(i, V)
            W(sc, (V, v_var))
            i = sc.activation(stat[:, 7:8], stat[:, 6:7], AF.Sqrt, bias=float(EPS_LN))
            ac_sd = kb.inc(i, AC)
            W(v, (AC, ac_sd))
            i = v.reciprocal(stat[:, 3:4], stat[:, 7:8])
            v_rs = kb.inc(i, V)
            W(v, [("V", v_rs), ("PE", prev_tr)])
            i = v.tensor_scalar(rhs[:, 0:H], hidt[:, :], stat[:, 2:3], stat[:, 3:4],
                                OP.subtract, OP.mult)
            v_ln = kb.inc(i, V)
            hid_free = v_ln
            for grp in range(4):
                W(t, [("V", v_ln), ("AC", ptr_free)])
                for ii in range(8):
                    hc = grp * 8 + ii
                    i = t.transpose(ptr[:, ii * 128:(ii + 1) * 128],
                                    rhs[:, hc * 128:(hc + 1) * 128], iden[:, :])
                prev_tr = kb.inc(i, PE)
                W(sc, [("PE", prev_tr), ("LS1", trev_free)])
                i = sc.copy(trev[:, :], ptr[:, :])
                ptr_free = kb.inc(i, AC)
                trev_free = kb.dma(
                    sy, "LS1",
                    ln1fm.ap()[grp * 1024:(grp + 1) * 1024, tb * 128:(tb + 1) * 128]
                    .rearrange("(hc p) t -> p hc t", p=128),
                    trev[:, :].rearrange("p (hc t) -> p hc t", t=128),
                    waits=[("AC", ptr_free)])
        ln1_st = kb.val("LS1")
        ptr_ctx.close()
        yac = pst("yac", [128, 1024], F32)
        l1ctx.close()
        hcs = sb("hcs", [128, NST * HF], BF16)

        if STAGE >= 1:
            W(g, [("LS1", ln1_st), ("GST0", gq_stage[0]), ("GST1", gq_stage[1]), ("GPAR", gq_small)])
            i = g.collective_compute("AllGather", OP.bypass,
                                     replica_groups=[list(range(NC))],
                                     ins=[ln1fm.ap()], outs=[ln1_all.ap()])
            kb.inc(i, CCAG)
        else:
            W(g, [("LS1", ln1_st), ("GST0", gq_stage[0]), ("GST1", gq_stage[1])])

        rhs_free = 0
        w_free = {0: 0, 1: 0}
        pp_free = {0: 0, 1: 0}
        xpp_free = 0
        cvp_free = 0
        cvp_free_v = [0]
        yac_free = {0: 0, 1: 0}
        xtr_free = {0: 0, 1: 0}
        xct_free = {0: [], 1: []}
        zct_free = {0: 0, 1: 0}
        eev_free = 0
        eev1_free = [0]
        e01_free = 0
        ebc_free = 0
        sq01_free = 0
        sqbc_free = 0
        msr_free = 0
        rro_free = 0
        rstb_free = 0
        rstbc_free = 0
        bcn_free = 0
        dtrn_free = 0
        brep_free = 0
        crep_free = 0
        dtp_free = {0: [], 1: []}
        dtt_free = {0: [], 1: []}
        xcs_free = {0: [], 1: []}
        zss_free = {0: 0, 1: 0}
        ybf_free = {0: 0, 1: 0}
        asc_free = {n_: 0 for n_ in range(4)}
        b0_free = {n_: 0 for n_ in range(4)}
        hc_free = {n_: 0 for n_ in range(NST)}
        dtx_free = {0: [], 1: []}
        rms_marks = {}
        dt_ready = {}
        xc_done = {}
        zs_done = {}
        vdt_prev = 0

        def emit_inproj_half(gg, t2, pieces=None, sched=()):
            nonlocal rhs_free, xpp_free, cvp_free, eev_free
            if pieces is None:
                pieces = []
            if gg == 0 and t2 == 0:
                W(sy, (CCAG, 1))
            for k in range(32):
                kb.dma(sy, "LRHS", rhs[:, k * 512:(k + 1) * 512],
                       bass.AP(ln1_all, k * 128 * TL + t2 * 4 * H * TL + gg * 128,
                               [[TL, 128], [H * TL, 4], [1, 128]]),
                       waits=[("PE", rhs_free)] if k == 0 else None)
            rhs_ld = kb.val("LRHS")
            for dtile in range(16):
                part = 0 if dtile < 8 else 1
                dt8 = dtile % 8
                wbuf = dtile % 2
                src = winx_bf if part == 0 else winz_bf
                w_ld = kb.dma(sy, f"LW{wbuf}",
                              wstr[:, wbuf * 4096:(wbuf + 1) * 4096],
                              src.ap()[dt8],
                              waits=[("PE", w_free[wbuf])])
                ppb = dtile % 2
                W(t, [(f"LW{wbuf}", w_ld), ("LRHS", rhs_ld), ("AC", pp_free[ppb])])
                if gg == 0 and t2 == 0 and dtile == 0:
                    W(t, [(CCAG, 1), ("V", v_diag)])
                for k in range(32):
                    i = t.matmul(pp[ppb][:, :],
                                 wstr[:, wbuf * 4096 + k * 128:wbuf * 4096 + (k + 1) * 128],
                                 rhs[:, k * 512:(k + 1) * 512],
                                 start=(k == 0), stop=(k == 31))
                pe_mm = kb.inc(i, PE)
                w_free[wbuf] = pe_mm
                if dtile == 15:
                    rhs_free = pe_mm

                if part == 0:
                    xb = dt8 % 2
                    W(sc, [("PE", pe_mm), ("PE", xtr_free[xb])])
                    i = sc.copy(xtr[:, xb * 528 + 16:xb * 528 + 528], pp[ppb][:, :])
                    ac_x = kb.inc(i, AC)
                    pp_free[ppb] = ac_x
                    W(g, (AC, ac_x))
                    g.tensor_copy(xtr[:, xb * 528 + 13:xb * 528 + 16],
                                  halo[:, dt8 * 4:dt8 * 4 + 3])
                    i = g.tensor_copy(halo[:, dt8 * 4:dt8 * 4 + 3],
                                      xtr[:, xb * 528 + 525:xb * 528 + 528])
                    pl_h = kb.inc(i, PL)
                    W(t, [("PL", pl_h), ("AC", cvp_free), ("V", cvp_free_v[0])])
                    for k in range(4):
                        i = t.matmul(cvp[:, :],
                                     convd[:, (dt8 * 4 + k) * 128:(dt8 * 4 + k + 1) * 128],
                                     xtr[:, xb * 528 + 13 + k:xb * 528 + 525 + k],
                                     start=(k == 0), stop=(k == 3))
                    pe_cv = kb.inc(i, PE)
                    xtr_free[xb] = pe_cv
                    cb = dt8 % 2
                    W(sc, [("PE", pe_cv)] + xct_free[cb])
                    i = sc.activation(xct[:, cb * 512:(cb + 1) * 512], cvp[:, :],
                                      AF.Silu, bias=cb_sb[:, dt8:dt8 + 1])
                    ac_xc = kb.inc(i, AC)
                    cvp_free = ac_xc
                    st = kb.dma(sc, f"LXC{cb}",
                                xc_dr.ap()[gg, dt8 * 128:(dt8 + 1) * 128,
                                           t2 * 512:(t2 + 1) * 512],
                                xct[:, cb * 512:(cb + 1) * 512], waits=[("AC", ac_xc)])
                    W(t, (AC, ac_xc))
                    if dt8 == 0:
                        W(t, (AC, xpp_free))
                    for e in range(3):
                        ecols = 128 if e < 2 else 32
                        i = t.matmul(xpp[e][0:ecols, :],
                                     wxp_sb[:, dt8 * 288 + e * 128:dt8 * 288 + e * 128 + ecols],
                                     xct[:, cb * 512:(cb + 1) * 512],
                                     start=(dt8 == 0), stop=(dt8 == 7))
                    pe_xp = kb.inc(i, PE)
                    xct_free[cb] = [("PE", pe_xp), (f"LXC{cb}", st)]
                if (t2 * 16 + dtile) in sched and pieces:
                    pieces.pop(0)()
                if part == 1:
                    zb = dt8 % 2
                    W(sc, [("PE", pe_mm), (f"LZS{zb}", zct_free[zb])])
                    i = sc.activation(zct[:, zb * 512:(zb + 1) * 512], pp[ppb][:, :], AF.Silu)
                    ac_z = kb.inc(i, AC)
                    pp_free[ppb] = ac_z
                    zct_free[zb] = kb.dma(
                        sc, f"LZS{zb}", zs_dr.ap()[gg, dt8 * 128:(dt8 + 1) * 128,
                                          t2 * 512:(t2 + 1) * 512],
                        zct[:, zb * 512:(zb + 1) * 512], waits=[("AC", ac_z)])
            if t2 == 1:
                while pieces:
                    pieces.pop(0)()
            pe_now = kb.val(PE)
            W(sc, [("PE", pe_now), ("LSM0", eev_free), ("LSM1", eev1_free[0])])
            for e in range(2):
                i = sc.copy(e_ev[:, e * 512:(e + 1) * 512], xpp[e][:, :])
            ac_ssm = kb.inc(i, AC)
            st0 = kb.dma(sc, "LSM0",
                         ssm_part.ap()[gg, 0:128, t2 * 512:(t2 + 1) * 512],
                         e_ev[:, 0:512], waits=[("AC", ac_ssm)])
            eev1_free[0] = kb.dma(sc, "LSM1",
                   ssm_part.ap()[gg, 128:256, t2 * 512:(t2 + 1) * 512],
                   e_ev[:, 512:1024])
            W(sc, ("LSM0", st0))
            i = sc.copy(e_ev[0:32, 0:512], xpp[2][0:32, :])
            ac_ssm2 = kb.inc(i, AC)
            xpp_free = ac_ssm2
            eev_free = kb.dma(sc, "LSM0",
                              ssm_part.ap()[gg, 256:288, t2 * 512:(t2 + 1) * 512],
                              e_ev[0:32, 0:512], waits=[("AC", ac_ssm2)])

        def rms_half(gg, sec, n_feat, half, cvp_row):
            nonlocal msr_free, rro_free
            W(sc, [("AC", msr_free)])
            i = sc.copy(msr[0:1, :], cvp_row)
            ac_ms = kb.inc(i, AC)
            W(sc, [("LRR", rro_free), ("AC", ac_ms)])
            i = sc.activation(rro[0:1, :], msr[0:1, :], AF.Sqrt,
                              scale=1.0 / n_feat, bias=float(EPS_RMS))
            ac_rt = kb.inc(i, AC)
            msr_free = ac_rt
            W(v, (AC, ac_rt))
            i = v.reciprocal(rro[0:1, :], rro[0:1, :])
            v_rc = kb.inc(i, V)
            rro_free = kb.dma(sc, "LRR",
                              r_rows.ap()[gg, sec:sec + 1, half * 512:(half + 1) * 512],
                              rro[0:1, :], waits=[("V", v_rc)])
            return ac_ms

        def emit_rms(gg):
            nonlocal cvp_free, e01_free, ebc_free, sq01_free, sqbc_free
            nonlocal rstb_free, rstbc_free, bcn_free, msr_free
            W(sy, (CCAR[gg], 1))
            kb.dma(sy, "LRM", e01[:, 0:TC], ssm_all.ap()[gg, 0:128, :],
                   waits=[("V", e01_free)])
            l01 = kb.dma(sy, "LRM", e01[:, TC:2 * TC], ssm_all.ap()[gg, 128:256, :])
            ac_sq = {}
            pe_ms = 0
            for e in range(2):
                W(sc, [("LRM", l01), ("PE", sq01_free if e == 0 else pe_ms)])
                i = sc.square(sq01[:, :], e01[:, e * TC:(e + 1) * TC])
                ac_sq[e] = kb.inc(i, AC)
                W(t, [("AC", ac_sq[e]), ("AC", cvp_free if e == 0 else 0),
                      ("V", cvp_free_v[0] if e == 0 else 0), ("AC", msr_free)])
                t.matmul(cvp[0:1, 0:512], ones128b[:, :], sq01[:, 0:512],
                         start=(e == 0), stop=(e == 1))
                i = t.matmul(cvp[64:65, 0:512], ones128b[:, :], sq01[:, 512:1024],
                             start=(e == 0), stop=(e == 1))
                pe_ms = kb.inc(i, PE)
                if e == 1:
                    sq01_free = pe_ms
            W(sc, (PE, pe_ms))
            rms_half(gg, 0, R, 0, cvp[0:1, 0:512])
            ac_ms2 = rms_half(gg, 0, R, 1, cvp[64:65, 0:512])
            cvp_free = ac_ms2
            st_dt = rro_free
            bl1 = kb.dma(sy, "LRBD", rstb[:, :],
                         r_rows.ap()[gg, 0:1, :].partition_broadcast(128),
                         waits=[("LRR", st_dt), ("V", rstb_free)])
            W(v, ("LRBD", bl1))
            for e in range(2):
                W(v, (PE, dtrn_free) if e == 0 else None)
                i = v.scalar_tensor_tensor(dtrn[:, e * TC:(e + 1) * TC],
                                           e01[:, e * TC:(e + 1) * TC],
                                           dtln_sb[:, e:e + 1], rstb[:, :],
                                           OP.mult, OP.mult)
            v_dtn = kb.inc(i, V)
            e01_free = v_dtn
            rstb_free = v_dtn
            for sec, (lnw, dst) in enumerate([(bln_sb, brow), (cln_sb, crow)]):
                lbc = kb.dma(sy, "LRMB", ebc[:, :],
                             ssm_all.ap()[gg, 256 + sec * NST:256 + (sec + 1) * NST, :],
                             waits=[("V", ebc_free)])
                W(sc, [("LRMB", lbc), ("PE", sqbc_free)])
                i = sc.square(sqbc[:, :], ebc[:, :])
                ac_sqb = kb.inc(i, AC)
                W(t, [("AC", ac_sqb), ("AC", cvp_free), ("V", cvp_free_v[0]), ("AC", msr_free)])
                t.matmul(cvp[0:1, 0:512], ones16b[:, :], sqbc[:, 0:512],
                         start=True, stop=True)
                i = t.matmul(cvp[64:65, 0:512], ones16b[:, :], sqbc[:, 512:1024],
                             start=True, stop=True)
                pe_msb = kb.inc(i, PE)
                sqbc_free = pe_msb
                W(sc, (PE, pe_msb))
                rms_half(gg, 1 + sec, NST, 0, cvp[0:1, 0:512])
                ac_ms2 = rms_half(gg, 1 + sec, NST, 1, cvp[64:65, 0:512])
                cvp_free = ac_ms2
                st_bc = rro_free
                bl = kb.dma(sy, "LRBC", rstbc[:, :],
                            r_rows.ap()[gg, 1 + sec:2 + sec, :].partition_broadcast(16),
                            waits=[("LRR", st_bc), ("V", rstbc_free)])
                W(v, [("LRBC", bl), ("LBC", bcn_free)])
                i = v.scalar_tensor_tensor(bcn[:, :], ebc[:, :],
                                           lnw[:, :], rstbc[:, :], OP.mult, OP.mult)
                v_bcn = kb.inc(i, V)
                rstbc_free = v_bcn
                ebc_free = v_bcn
                bcn_free = kb.dma(sc, "LBC", dst.ap()[gg], bcn[:, :], waits=[("V", v_bcn)])
            rms_marks[gg] = dict(v_dtn=v_dtn, brow_st=bcn_free)

        def emit_dtproj_part(gg, dcs):
            nonlocal cvp_free, dtrn_free, vdt_prev
            m = rms_marks[gg]
            for dc in dcs:
                for t2 in range(2):
                    dbuf = t2
                    W(t, [("V", m["v_dtn"]), ("AC", cvp_free), ("V", cvp_free_v[0])])
                    for rc in range(2):
                        t.matmul(cvp[:, :],
                                 wdt_sb[:, rc * DL + dc * 128:rc * DL + (dc + 1) * 128],
                                 dtrn[:, rc * TC + t2 * 512:rc * TC + (t2 + 1) * 512],
                                 start=(rc == 0), stop=False)
                    i = t.matmul(cvp[:, :], dtbd[:, dc * 128:(dc + 1) * 128],
                                 ones512b[:, :], start=False, stop=True)
                    pe_dt = kb.inc(i, PE)
                    dtrn_free = pe_dt
                    W(sc, [("PE", pe_dt), ("V", vdt_prev)])
                    i = sc.activation(spt[:, 512:1024], cvp[:, :], AF.Abs)
                    a1 = kb.inc(i, AC)
                    cvp_free = a1
                    W(sc, (AC, a1))
                    i = sc.activation(spt[:, 512:1024], spt[:, 512:1024], AF.Exp, scale=-1.0)
                    a2 = kb.inc(i, AC)
                    W(sc, (AC, a2))
                    i = sc.activation(spt[:, 512:1024], spt[:, 512:1024], AF.Ln, bias=1.0)
                    ac_l = kb.inc(i, AC)
                    W(v, [("AC", ac_l), ("PE", pe_dt)] + dtp_free[dbuf])
                    i = v.tensor_scalar_max(spt[:, 0:512], cvp[:, :], 0.0)
                    v_mx = kb.inc(i, V)
                    cvp_free_v[0] = v_mx
                    W(v, (V, v_mx))
                    i = v.tensor_add(dtp_t[:, dbuf * 512:(dbuf + 1) * 512],
                                     spt[:, 0:512], spt[:, 512:1024])
                    vdt_prev = kb.inc(i, V)
                    st = kb.dma(sc, f"LDT{dbuf}",
                                dt_dr.ap()[gg, dc * 128:(dc + 1) * 128,
                                           t2 * 512:(t2 + 1) * 512],
                                dtp_t[:, dbuf * 512:(dbuf + 1) * 512],
                                waits=[("V", vdt_prev)])
                    dtp_free[dbuf] = [(f"LDT{dbuf}", st)]
            if 7 in dcs:
                dt_ready[gg] = (kb.val("LDT0"), kb.val("LDT1"))

        def emit_dtproj(gg):
            emit_dtproj_part(gg, range(0, 8))

        def make_scan_pieces(gg):
            st_ = {"rep_ld": 0}
            pieces = []

            def piece(hf, dc):
                nonlocal brep_free, crep_free
                db = dc % 2
                if dc == 0:
                    m = rms_marks[gg]
                    W(sy, [("LBC", m["brow_st"]), ("PL", brep_free), ("V", crep_free)])
                    for n in range(NST):
                        kb.dma(sy, "LRP", brep[:, n * HF:(n + 1) * HF],
                               brow.ap()[gg, n:n + 1, hf * HF:(hf + 1) * HF]
                               .partition_broadcast(128))
                    for n in range(NST):
                        kb.dma(sy, "LRP", crep[:, n * HF:(n + 1) * HF],
                               crow.ap()[gg, n:n + 1, hf * HF:(hf + 1) * HF]
                               .partition_broadcast(128))
                    st_["rep_ld"] = kb.val("LRP")
                rep_ld = st_["rep_ld"]
                if True:
                    dl_ = kb.dma(sy, f"LSCD{db}", dtt[:, db * HF:(db + 1) * HF],
                                 dt_dr.ap()[gg, dc * 128:(dc + 1) * 128,
                                            hf * HF:(hf + 1) * HF],
                                 waits=[("LDT0", dt_ready[gg][0]),
                                        ("LDT1", dt_ready[gg][1]),
                                        ("LXC0", xc_done[gg][0]),
                                        ("LXC1", xc_done[gg][1]),
                                        ("LZS0", zs_done[gg][0]),
                                        ("LZS1", zs_done[gg][1])] + dtt_free[db])
                    xl_ = kb.dma(sy, f"LSCX{db}", xcs[:, db * HF:(db + 1) * HF],
                                 xc_dr.ap()[gg, dc * 128:(dc + 1) * 128,
                                            hf * HF:(hf + 1) * HF],
                                 waits=xcs_free[db])
                    zl_ = kb.dma(sy, f"LSCZ{db}", zss[:, db * HF:(db + 1) * HF],
                                 zs_dr.ap()[gg, dc * 128:(dc + 1) * 128,
                                            hf * HF:(hf + 1) * HF],
                                 waits=[("V", zss_free[db])])
                    W(v, [(f"LSCD{db}", dl_), (f"LSCX{db}", xl_)] + dtx_free[db])
                    i = v.tensor_mul(dtxb[:, db * HF:(db + 1) * HF],
                                     dtt[:, db * HF:(db + 1) * HF],
                                     xcs[:, db * HF:(db + 1) * HF])
                    v_dtx = kb.inc(i, V)
                    ac_agen_last = 0
                    pl_b0_last = 0
                    v_ym_last = 0
                    for n in range(NST):
                        ab = n % 2
                        as_ = n % 4
                        W(sc, [(f"LSCD{db}", dl_), ("V", asc_free[as_])])
                        i = sc.activation(asc[:, as_ * HF:(as_ + 1) * HF],
                                          dtt[:, db * HF:(db + 1) * HF], AF.Exp,
                                          scale=a_sb[:, dc * NST + n:dc * NST + n + 1])
                        ac_agen_last = kb.inc(i, AC)
                        W(g, [("V", v_dtx), ("LRP", rep_ld), ("V", b0_free[as_])])
                        i = g.tensor_mul(b0s[:, as_ * HF:(as_ + 1) * HF],
                                         dtxb[:, db * HF:(db + 1) * HF],
                                         brep[:, n * HF:(n + 1) * HF])
                        pl_b0_last = kb.inc(i, PL)
                        W(v, [("AC", ac_agen_last), ("PL", pl_b0_last)])
                        i = v.tensor_tensor_scan(hsc[:, ab * HF:(ab + 1) * HF],
                                                 asc[:, as_ * HF:(as_ + 1) * HF],
                                                 b0s[:, as_ * HF:(as_ + 1) * HF],
                                                 carry[:, dc * NST + n:dc * NST + n + 1],
                                                 OP.mult, OP.add)
                        v_h = kb.inc(i, V)
                        asc_free[as_] = v_h
                        b0_free[as_] = v_h
                        W(v, (V, v_h))
                        v.tensor_copy(carry[:, dc * NST + n:dc * NST + n + 1],
                                      hsc[:, (ab + 1) * HF - 1:(ab + 1) * HF])
                        W(v, [("PE", hc_free[n]), ("LRP", rep_ld)])
                        i = v.tensor_mul(hcs[:, n * HF:(n + 1) * HF],
                                         hsc[:, ab * HF:(ab + 1) * HF],
                                         crep[:, n * HF:(n + 1) * HF])
                        v_ym_last = kb.inc(i, V)
                        W(t, (V, v_ym_last))
                        if n == 0:
                            W(t, [("V", yac_free[db]), (f"LSCX{db}", xl_)])
                        i = t.matmul(yac[:, db * 512:(db + 1) * 512], iden[:, :],
                                     hcs[:, n * HF:(n + 1) * HF],
                                     start=(n == 0), stop=False)
                        pe_y = kb.inc(i, PE)
                        hc_free[n] = pe_y
                    i = t.matmul(yac[:, db * 512:(db + 1) * 512],
                                 dprd[:, dc * 128:(dc + 1) * 128],
                                 xcs[:, db * HF:(db + 1) * HF],
                                 start=False, stop=True)
                    pe_yd = kb.inc(i, PE)
                    xcs_free[db] = [("PE", pe_yd), ("V", v_dtx)]
                    dtt_free[db] = [("AC", ac_agen_last), ("V", v_dtx)]
                    dtx_free[db] = [("PL", pl_b0_last)]
                    W(v, [("PE", pe_yd), (f"LSCZ{db}", zl_), (f"LYB{db}", ybf_free[db])])
                    i = v.tensor_mul(ybf[:, db * HF:(db + 1) * HF],
                                     yac[:, db * 512:(db + 1) * 512],
                                     zss[:, db * HF:(db + 1) * HF])
                    v_y = kb.inc(i, V)
                    yac_free[db] = v_y
                    zss_free[db] = v_y
                    W(v, (V, v_y))
                    ybf_free[db] = kb.dma(
                        sc, f"LYB{db}",
                        bass.AP(a2a_in, gg * DIN * 128 + dc * 128 * 128 + hf * 4 * DL * 128,
                                [[128, 128], [DL * 128, 4], [1, 128]]),
                        ybf[:, db * HF:(db + 1) * HF], waits=[("V", v_y)])
                if dc == 7:
                    brep_free = pl_b0_last
                    crep_free = v_ym_last
                if hf == 1 and dc == 7:
                    W(g, [("LYB0", kb.val("LYB0")), ("LYB1", kb.val("LYB1"))])
                    with allow_ici_all_to_all():
                        i2 = g.collective_compute(
                            "AllToAll", OP.bypass,
                            replica_groups=[list(range(NC))],
                            ins=[a2a_in.ap()[gg]], outs=[a2a_out.ap()[gg]])
                    kb.inc(i2, CCA2A[gg])

            for hf in range(2):
                for dc in range(8):
                    pieces.append(lambda hf=hf, dc=dc: piece(hf, dc))
            return pieces

        for gg in range(NG if STAGE >= 2 else 0):
            pieces = []
            if gg >= 1 and STAGE >= 3:
                pieces.append(lambda gp=gg - 1: emit_rms(gp))
                pieces.append(lambda gp=gg - 1: emit_dtproj_part(gp, range(0, 4)))
                pieces.append(lambda gp=gg - 1: emit_dtproj_part(gp, range(4, 8)))
                if STAGE >= 4:
                    pieces += make_scan_pieces(gg - 1)
            np_ = len(pieces)
            sched = {d for d in range(32)
                     if (d + 1) * np_ // 32 > d * np_ // 32}
            emit_inproj_half(gg, 0, pieces, sched)
            emit_inproj_half(gg, 1, pieces, sched)
            xc_done[gg] = (kb.val("LXC0"), kb.val("LXC1"))
            zs_done[gg] = (kb.val("LZS0"), kb.val("LZS1"))
            W(g, [("LSM0", kb.val("LSM0")), ("LSM1", kb.val("LSM1"))])
            i = g.collective_compute("AllReduce", OP.add, replica_groups=[list(range(NC))],
                                     ins=[ssm_part.ap()[gg]], outs=[ssm_all.ap()[gg]])
            kb.inc(i, CCAR[gg])
        if STAGE >= 3:
            emit_rms(NG - 1)
            emit_dtproj(NG - 1)
        if STAGE >= 4:
            for p in make_scan_pieces(NG - 1):
                p()

        if DEBUG:
            for eng_nm in ("LXC0", "LXC1", "LZS0", "LZS1", "LDT0", "LDT1",
                           "LYB0", "LYB1", "LS1", "LBC", "LRR"):
                W(sy, (eng_nm, kb.val(eng_nm)))
            if STAGE >= 1:
                W(sy, (CCAG, 1))
            for gg2 in range(NG if STAGE >= 2 else 0):
                W(sy, (CCAR[gg2], 1))
            for nm, srct in [("dbg_xc", xc_dr), ("dbg_dt", dt_dr), ("dbg_y", a2a_in),
                             ("dbg_ssm", ssm_all), ("dbg_ln1", ln1_all),
                             ("dbg_zs", zs_dr), ("dbg_ln1my", ln1fm),
                             ("dbg_brow", brow), ("dbg_crow", crow),
                             ("dbg_rr", r_rows)]:
                kb.dma(sy, L, dbg[nm].ap(), srct.ap())

        mix_sems = ["PE", "V", "AC", "PL", "GQ", "L", "GPAR", "GST0", "GST1", "LH",
                    "LS1", "LRHS", "LW0", "LW1", "LXC0", "LXC1", "LZS0",
                    "LZS1", "LSM0", "LSM1", "LRM", "LRMB", "LRR", "LRBD",
                    "LRBC", "LBC", "LDT0", "LDT1", "LRP", "LSCD0", "LSCD1",
                    "LSCX0", "LSCX1", "LSCZ0", "LSCZ1", "LYB0", "LYB1"]
        mix = {nm: kb.val(nm) for nm in mix_sems}
        mctx.close()

        tctx = ExitStack()
        sb2 = lambda n, s, d: tctx.enter_context(nc.sbuf_tensor(n, s, d))

        iden2 = sb2("iden2", [128, 128], BF16)
        idenf2 = sb2("idenf2", [128, 128], F32)
        ymine = sb2("ymine", [128, 64 * 1024], BF16)
        wo_t = sb2("wo_t", [128, 2 * 512], BF16)
        hid_t = sb2("hid_t", [128, 2 * 512], F32)
        x2t = sb2("x2t", [128, 2 * 512], F32)
        x2sq = sb2("x2sq", [128, 512], BF16)
        ln2st = sb2("ln2st", [128, 8 * 8], F32)
        ln2t = sb2("ln2t", [128, 512], BF16)
        trev2 = sb2("trev2", [128, 512], BF16)
        perm_sb = sb2("perm_sb", [128, 32], I32)
        ln2p = sb2("ln2p", [128, 2 * 1024], BF16)
        upw = sb2("upw", [128, 2048], BF16)
        dnw = sb2("dnw", [128, 2048], BF16)
        t2sb = sb2("t2sb", [128, 8 * 1024], BF16)
        ffnev = sb2("ffnev", [128, 2 * 1024], F32)
        ffnt = sb2("ffnt", [128, 1024], F32)
        x2c = sb2("x2c", [128, 2 * 1024], F32)
        outt = sb2("outt", [128, 2 * 1024], F32)

        for eng in (sy, v, sc, t, g):
            W(eng, [(nm, mix[nm]) for nm in mix_sems])
            if STAGE >= 4:
                for gg in range(NG):
                    W(eng, (CCA2A[gg], 1))

        g.memset(iden2[:, :], 0.0)
        g.affine_select(out=iden2[:, :], in_=iden2[:, :], compare_op=OP.not_equal,
                        fill=1.0, base=0, pattern=[[-1, 128]], channel_multiplier=1)
        g.memset(idenf2[:, :], 0.0)
        i = g.affine_select(out=idenf2[:, :], in_=idenf2[:, :], compare_op=OP.not_equal,
                            fill=1.0, base=0, pattern=[[-1, 128]], channel_multiplier=1)
        gq_id2 = kb.inc(i, GQ, 16)
        for gi in range(32):
            kb.dma(g, "GPM", perm_sb[:, gi:gi + 1], perm_in.ap()[gi * 128:(gi + 1) * 128, :])
        gq_perm = kb.val("GPM")

        for gg in range(NG if STAGE >= 5 else 0):
            kb.dma(sy, "LYM",
                   ymine[:, :].rearrange("p (dcc tt) -> p dcc tt", tt=1024)
                   [:, :, gg * 128:(gg + 1) * 128],
                   bass.AP(a2a_out, gg * DIN * 128, [[128, 128], [16384, 64], [1, 128]]))
        ym_ld = kb.val("LYM")

        opctx = ExitStack()
        op = [opctx.enter_context(nc.psum_tensor(f"op{i_}", [128, 512], F32))
              for i_ in range(8)]
        op_free = {i_: 0 for i_ in range(8)}
        x2sq_free = [0]
        wo_free = {0: 0, 1: 0}
        hid_free2 = {0: 0, 1: 0}
        x2t_free = {0: 0, 1: 0}
        pe_op = 0
        for hb in range(8 if STAGE >= 5 else 0):
            for dc in range(64):
                wb = dc % 2
                wl = kb.dma(g, f"GWO{wb}", wo_t[:, wb * 512:(wb + 1) * 512],
                            woutT.ap()[dc * 128:(dc + 1) * 128, hb * 512:(hb + 1) * 512],
                            waits=[("PE", wo_free[wb])])
                W(t, [(f"GWO{wb}", wl), ("LYM", ym_ld)])
                for tt in range(8):
                    if dc == 0:
                        W(t, (V, op_free[tt]))
                    i = t.matmul(op[tt][:, :],
                                 ymine[:, dc * 1024 + tt * 128:dc * 1024 + (tt + 1) * 128],
                                 wo_t[:, wb * 512:(wb + 1) * 512],
                                 start=(dc == 0), stop=(dc == 63))
                pe_op = kb.inc(i, PE)
                wo_free[wb] = pe_op
            for tt in range(8):
                hbuf = tt % 2
                hl = kb.dma(sy, f"LHT{hbuf}", hid_t[:, hbuf * 512:(hbuf + 1) * 512],
                            hid_my.ap()[tt * 128:(tt + 1) * 128, hb * 512:(hb + 1) * 512],
                            waits=[("V", hid_free2[hbuf])])
                W(v, [("PE", pe_op), (f"LHT{hbuf}", hl), (f"LXW{hbuf}", x2t_free[hbuf])])
                i = v.tensor_add(x2t[:, hbuf * 512:(hbuf + 1) * 512], op[tt][:, :],
                                 hid_t[:, hbuf * 512:(hbuf + 1) * 512])
                v_x2 = kb.inc(i, V)
                hid_free2[hbuf] = v_x2
                op_free[tt] = v_x2
                W(sc, [("V", v_x2), ("V", x2sq_free[0])])
                i = sc.activation(x2sq[:, :], x2t[:, hbuf * 512:(hbuf + 1) * 512],
                                  AF.Square)
                ac_q = kb.inc(i, AC)
                i = v.reduce_sum(ln2st[:, tt * 8 + 5:tt * 8 + 6],
                                 x2t[:, hbuf * 512:(hbuf + 1) * 512], AX.X)
                v_s1 = kb.inc(i, V)
                W(v, [("AC", ac_q), ("V", v_s1)])
                i = v.reduce_sum(ln2st[:, tt * 8 + 4:tt * 8 + 5], x2sq[:, :], AX.X)
                v_s2 = kb.inc(i, V)
                x2sq_free[0] = v_s2
                W(v, (V, v_s2))
                if hb == 0:
                    v.tensor_copy(ln2st[:, tt * 8 + 0:tt * 8 + 1],
                                  ln2st[:, tt * 8 + 5:tt * 8 + 6])
                    i = v.tensor_copy(ln2st[:, tt * 8 + 1:tt * 8 + 2],
                                      ln2st[:, tt * 8 + 4:tt * 8 + 5])
                else:
                    v.tensor_add(ln2st[:, tt * 8 + 0:tt * 8 + 1],
                                 ln2st[:, tt * 8 + 0:tt * 8 + 1],
                                 ln2st[:, tt * 8 + 5:tt * 8 + 6])
                    i = v.tensor_add(ln2st[:, tt * 8 + 1:tt * 8 + 2],
                                     ln2st[:, tt * 8 + 1:tt * 8 + 2],
                                     ln2st[:, tt * 8 + 4:tt * 8 + 5])
                v_st = kb.inc(i, V)
                x2t_free[hbuf] = kb.dma(
                    sy, f"LXW{hbuf}", x2_dr.ap()[tt * 128:(tt + 1) * 128, hb * 512:(hb + 1) * 512],
                    x2t[:, hbuf * 512:(hbuf + 1) * 512], waits=[("V", v_st)])
        x2_all_st = (kb.val("LXW0"), kb.val("LXW1"))

        W(v, (V, kb.val("V")))
        for tt in range(8 if STAGE >= 5 else 0):
            c0 = tt * 8
            v.tensor_scalar_mul(ln2st[:, c0 + 2:c0 + 3], ln2st[:, c0 + 0:c0 + 1], 1.0 / H)
            i = v.tensor_scalar_mul(ln2st[:, c0 + 3:c0 + 4], ln2st[:, c0 + 1:c0 + 2], 1.0 / H)
            v_f = kb.inc(i, V)
            W(v, (V, v_f))
            i = v.tensor_mul(ln2st[:, c0 + 6:c0 + 7], ln2st[:, c0 + 2:c0 + 3],
                             ln2st[:, c0 + 2:c0 + 3])
            v_f2 = kb.inc(i, V)
            W(v, (V, v_f2))
            i = v.tensor_sub(ln2st[:, c0 + 7:c0 + 8], ln2st[:, c0 + 3:c0 + 4],
                             ln2st[:, c0 + 6:c0 + 7])
            v_v = kb.inc(i, V)
            W(sc, (V, v_v))
            i = sc.activation(ln2st[:, c0 + 6:c0 + 7], ln2st[:, c0 + 7:c0 + 8],
                              AF.Sqrt, bias=float(EPS_LN))
            ac_s = kb.inc(i, AC)
            W(v, (AC, ac_s))
            i = v.reciprocal(ln2st[:, c0 + 7:c0 + 8], ln2st[:, c0 + 6:c0 + 7])
            kb.inc(i, V)
        v_ln2fin = kb.val(V)

        opctx.close()
        p2ctx = ExitStack()
        ptr2 = p2ctx.enter_context(nc.psum_tensor("ptr2", [128, 512], BF16))
        pfm = [p2ctx.enter_context(nc.psum_tensor("pfm0", [128, 512], F32)),
               p2ctx.enter_context(nc.psum_tensor("pfm1", [128, 512], F32))]
        pt2 = p2ctx.enter_context(nc.psum_tensor("pt2", [128, 1024], F32))

        ptr2_free = 0
        trev2_free = 0
        x2f_free = {0: 0, 1: 0}
        ln2t_pe = 0
        for tt in range(8 if STAGE >= 5 else 0):
            for hb in range(8):
                fb = hb % 2
                xl = kb.dma(sy, f"LXR{fb}", hid_t[:, fb * 512:(fb + 1) * 512],
                            x2_dr.ap()[tt * 128:(tt + 1) * 128, hb * 512:(hb + 1) * 512],
                            waits=[("V", x2f_free[fb]), ("LXW0", x2_all_st[0]),
                                   ("LXW1", x2_all_st[1]), ("V", hid_free2[fb])])
                W(v, [(f"LXR{fb}", xl), ("V", v_ln2fin), ("PE", ln2t_pe)])
                i = v.tensor_scalar(ln2t[:, :], hid_t[:, fb * 512:(fb + 1) * 512],
                                    ln2st[:, tt * 8 + 2:tt * 8 + 3],
                                    ln2st[:, tt * 8 + 7:tt * 8 + 8],
                                    OP.subtract, OP.mult)
                v_n = kb.inc(i, V)
                x2f_free[fb] = v_n
                W(t, [("V", v_n), ("AC", ptr2_free)])
                for s in range(4):
                    i = t.transpose(ptr2[:, s * 128:(s + 1) * 128],
                                    ln2t[:, s * 128:(s + 1) * 128], iden2[:, :])
                ln2t_pe = kb.inc(i, PE)
                W(sc, [("PE", ln2t_pe), ("LT2", trev2_free)])
                i = sc.copy(trev2[:, :], ptr2[:, :])
                ptr2_free = kb.inc(i, AC)
                trev2_free = kb.dma(
                    sy, "LT2",
                    ln2fm.ap()[hb * 512:(hb + 1) * 512, tt * 128:(tt + 1) * 128]
                    .rearrange("(s p) t -> p s t", p=128),
                    trev2[:, :].rearrange("p (s t) -> p s t", t=128),
                    waits=[("AC", ptr2_free)])
        ln2_st = kb.val("LT2")
        if DEBUG:
            kb.dma(sy, L, dbg["dbg_ln2"].ap(), ln2fm.ap(), waits=[("LT2", ln2_st)])
            kb.dma(sy, L, dbg["dbg_x2"].ap(), x2_dr.ap(), waits=[("LXW0", x2_all_st[0]), ("LXW1", x2_all_st[1])])

        pfm_free = [[], []]
        ffn_sc_done = [0]
        gath_free = {0: 0, 1: 0}
        upw_free = {0: 0, 1: 0}
        dnw_free = {0: 0, 1: 0}
        t2_free = 0
        ffn_ev_free = {0: 0, 1: 0}
        for h in range(NST if STAGE >= 6 else 0):
            ul = kb.dma(g, "GFW", upw[:, :].rearrange("p (kc m) -> p kc m", m=1024),
                        up_w.ap()[h].rearrange("(kc p) m -> p kc m", p=128),
                        waits=[("PE", upw_free[0]), ("GPM", gq_perm)])
            dl_ = kb.dma(g, "GFW", dnw[:, :].rearrange("p (kc d) -> p kc d", d=256),
                         dn_w.ap()[h].rearrange("(kc p) d -> p kc d", p=128),
                         waits=[("PE", dnw_free[0])])
            for kc in range(2):
                W(g, [("LT2", ln2_st), ("PE", gath_free[0])])
                i = g.indirect_dma_start(
                    out=ln2p[:, kc * 1024:(kc + 1) * 1024],
                    out_offset=None,
                    in_=ln2fm.ap()[0:128, :],
                    in_offset=IndirectOffsetOnAxis(perm_sb[:, 2 * h + kc:2 * h + kc + 1], 0))
                kb.inc(i, "GGA", 16)
            gl = kb.val("GGA")
            for mc in range(8):
                for t2 in range(2):
                    pb = (mc * 2 + t2) % 2
                    W(t, [("GGA", gl), ("GFW", dl_)] + pfm_free[pb])
                    for kc in range(2):
                        i = t.matmul(pfm[pb][:, :],
                                     upw[:, kc * 1024 + mc * 128:kc * 1024 + (mc + 1) * 128],
                                     ln2p[:, kc * 1024 + t2 * 512:kc * 1024 + (t2 + 1) * 512],
                                     start=(kc == 0), stop=(kc == 1))
                    pe_u = kb.inc(i, PE)
                    if mc == 7 and t2 == 1:
                        upw_free[0] = pe_u
                        gath_free[0] = pe_u
                    W(sc, [("PE", pe_u), ("PE", t2_free)])
                    i = sc.activation(t2sb[:, mc * 1024 + t2 * 512:mc * 1024 + (t2 + 1) * 512],
                                      pfm[pb][:, :], AF.Gelu)
                    ac_g2 = kb.inc(i, AC)
                    pfm_free[pb] = [("AC", ac_g2)]
            ac_t2 = kb.val(AC)
            for dsub in range(2):
                for t2 in range(2):
                    pb = (dsub * 2 + t2) % 2
                    W(t, [("AC", ac_t2), ("GFW", dl_), ("GSC", ffn_sc_done[0])]
                      + pfm_free[pb])
                    for kc in range(8):
                        i = t.matmul(pfm[pb][:, :],
                                     dnw[:, kc * 256 + dsub * 128:kc * 256 + (dsub + 1) * 128],
                                     t2sb[:, kc * 1024 + t2 * 512:kc * 1024 + (t2 + 1) * 512],
                                     start=(kc == 0), stop=(kc == 7))
                    pe_d = kb.inc(i, PE)
                    if dsub == 1 and t2 == 1:
                        t2_free = pe_d
                        dnw_free[0] = pe_d
                    W(v, (PE, pe_d))
                    i = v.tensor_copy(ffnev[:, dsub * 1024 + t2 * 512:
                                            dsub * 1024 + (t2 + 1) * 512],
                                      pfm[pb][:, :])
                    v_e = kb.inc(i, V)
                    pfm_free[pb] = [("V", v_e)]
            v_last = kb.val(V)
            for dsub in range(2):
                W(g, (V, v_last))
                i = g.indirect_dma_start(
                    out=ffn_dr.ap()[0:128, :],
                    out_offset=IndirectOffsetOnAxis(perm_sb[:, 2 * h + dsub:2 * h + dsub + 1], 0),
                    in_=ffnev[:, dsub * 1024:(dsub + 1) * 1024],
                    in_offset=None)
                kb.inc(i, "GSC", 16)
            ffn_sc_done[0] = kb.val("GSC")
        ffn_done = kb.val("GSC")
        if DEBUG:
            kb.dma(sy, L, dbg["dbg_ffn"].ap(), ffn_dr.ap(), waits=[("GSC", ffn_done)])

        pt2_free = 0
        ffnt_free = {0: 0, 1: 0}
        x2c_free = {0: 0, 1: 0}
        outt_free = {0: 0, 1: 0}
        for cc2 in range(32 if STAGE >= 6 else 0):
            fb = cc2 % 2
            fl = kb.dma(sy, "LFT", ffnt[:, :],
                        ffn_dr.ap()[cc2 * 128:(cc2 + 1) * 128, :],
                        waits=[("GSC", ffn_done), ("PE", ffnt_free[0])])
            W(t, [("LFT", fl), ("V", pt2_free)])
            for tt in range(8):
                i = t.transpose(pt2[:, tt * 128:(tt + 1) * 128],
                                ffnt[:, tt * 128:(tt + 1) * 128],
                                idenf2[:, :])
            pe_ft = kb.inc(i, PE)
            ffnt_free[0] = pe_ft
            xl = kb.dma(sy, f"LXC2{fb}",
                        x2c[:, fb * 1024:(fb + 1) * 1024]
                        .rearrange("p (tt c) -> p tt c", c=128),
                        x2_dr.ap()[:, cc2 * 128:(cc2 + 1) * 128]
                        .rearrange("(tt p) c -> p tt c", p=128),
                        waits=[("V", x2c_free[fb])])
            W(v, [("PE", pe_ft), (f"LXC2{fb}", xl), (f"LOU{fb}", outt_free[fb])])
            i = v.tensor_add(outt[:, fb * 1024:(fb + 1) * 1024], pt2[:, :],
                             x2c[:, fb * 1024:(fb + 1) * 1024])
            v_o = kb.inc(i, V)
            pt2_free = v_o
            x2c_free[fb] = v_o
            W(v, (V, v_o))
            outt_free[fb] = kb.dma(
                sy, f"LOU{fb}",
                out_my.ap()[:, cc2 * 128:(cc2 + 1) * 128]
                .rearrange("(tt p) c -> p tt c", p=128),
                outt[:, fb * 1024:(fb + 1) * 1024]
                .rearrange("p (tt c) -> p tt c", c=128),
                waits=[("V", v_o)])
        for _nm in ("LOU0", "LOU1", "LXW0", "LXW1", "LT2", "LS1"):
            W(sy, (_nm, kb.val(_nm)))
        for _nm in ("GSC", "GGA", "GFW", "GWO0", "GWO1", "GPM"):
            W(g, (_nm, kb.val(_nm)))
        p2ctx.close()
        tctx.close()
        pctx.close()
    kb.ctx.close()
    return kb


def kernel(L0_attn_hidden_states, L0_attn_in_proj_w, L0_attn_conv_w, L0_attn_conv_b,
           L0_attn_x_proj_w, L0_attn_dt_proj_w, L0_attn_dt_bias, L0_attn_A_log,
           L0_attn_D_param, L0_attn_out_proj_w, L0_attn_dt_ln_w, L0_attn_B_ln_w,
           L0_attn_C_ln_w, L0_ffn_up, L0_ffn_down, L0_ffn_perm):
    hid = np.ascontiguousarray(np.asarray(L0_attn_hidden_states, dtype=np.float32)
                               .reshape(T, H))
    win = np.asarray(L0_attn_in_proj_w, dtype=np.float32)
    wxpT_h = np.ascontiguousarray(np.asarray(L0_attn_x_proj_w, dtype=np.float32).T)
    wdtT_h = np.ascontiguousarray(np.asarray(L0_attn_dt_proj_w, dtype=np.float32).T)
    woutT_h = np.ascontiguousarray(np.asarray(L0_attn_out_proj_w, dtype=np.float32).T)
    convw = np.ascontiguousarray(np.asarray(L0_attn_conv_w, dtype=np.float32))
    convb = np.asarray(L0_attn_conv_b, dtype=np.float32).reshape(DIN, 1)
    dtb = np.asarray(L0_attn_dt_bias, dtype=np.float32).reshape(DIN, 1)
    alog = np.ascontiguousarray(np.asarray(L0_attn_A_log, dtype=np.float32))
    dprm = np.asarray(L0_attn_D_param, dtype=np.float32).reshape(DIN, 1)
    dtln = np.asarray(L0_attn_dt_ln_w, dtype=np.float32).reshape(R, 1)
    bln = np.asarray(L0_attn_B_ln_w, dtype=np.float32).reshape(NST, 1)
    cln = np.asarray(L0_attn_C_ln_w, dtype=np.float32).reshape(NST, 1)
    upw_h = np.ascontiguousarray(np.asarray(L0_ffn_up, dtype=np.float32))
    dnw_h = np.ascontiguousarray(np.asarray(L0_ffn_down, dtype=np.float32))
    perm = np.ascontiguousarray(np.asarray(L0_ffn_perm, dtype=np.int32).reshape(H, 1))

    hid_blocks = hid.reshape(64, 128, H)
    kb = build()
    in_maps = []
    for k in range(NC):
        in_maps.append({
            "hid_my": np.ascontiguousarray(hid_blocks[k::8].reshape(TL, H)),
            "win_x": np.ascontiguousarray(win[:, k * DL:(k + 1) * DL]),
            "win_z": np.ascontiguousarray(win[:, DIN + k * DL:DIN + (k + 1) * DL]),
            "wxpT": np.ascontiguousarray(wxpT_h[k * DL:(k + 1) * DL, :]),
            "wdtT": np.ascontiguousarray(wdtT_h[:, k * DL:(k + 1) * DL]),
            "woutT": woutT_h,
            "conv_w": np.ascontiguousarray(
                convw[k * DL:(k + 1) * DL, :].reshape(8, 128, 4)
                .transpose(1, 0, 2).reshape(128, 32)),
            "conv_b": np.ascontiguousarray(
                convb[k * DL:(k + 1) * DL, 0].reshape(8, 128).T),
            "dt_bias": np.ascontiguousarray(
                dtb[k * DL:(k + 1) * DL, 0].reshape(8, 128).T),
            "a_log": np.ascontiguousarray(
                alog[k * DL:(k + 1) * DL, :].reshape(8, 128, NST)
                .transpose(1, 0, 2).reshape(128, 8 * NST)),
            "d_prm": np.ascontiguousarray(
                dprm[k * DL:(k + 1) * DL, 0].reshape(8, 128).T),
            "dt_ln": np.ascontiguousarray(dtln[:, 0].reshape(2, 128).T),
            "b_ln": bln, "c_ln": cln,
            "up_w": upw_h, "dn_w": dnw_h, "perm_in": perm,
        })
    res = run_bass_kernel_spmd(kb.nc, in_maps, core_ids=list(range(NC)))
    out = np.zeros((64, 128, H), dtype=np.float32)
    for k in range(NC):
        out[k::8] = np.asarray(res.results[k]["out_my"]).reshape(8, 128, H)
    return out.reshape(1, T, H)
